# version 03
# baseline (speedup 1.0000x reference)
import numpy as np
import ml_dtypes
import concourse.bass as bass
import concourse.mybir as mybir
from concourse.bass_utils import run_bass_kernel_spmd
from contextlib import ExitStack

F32 = mybir.dt.float32
BF16 = mybir.dt.bfloat16
I32 = mybir.dt.int32
AF = mybir.ActivationFunctionType
ALU = mybir.AluOpType

NCORES = 8
SEQ = 2048
HID = 4096
KC = HID // 128
TT = 512
NTILES = SEQ // TT
CH = 128
CPT = TT // CH
FFI = 11008
FC = FFI // 128
MIN = 18560
MCI = MIN // 128
IM = 8192
CCN = IM // 128
CONVD = 10240
CVC = CONVD // 128
NG = 8
PD = 64
EPS = 1e-5

ENGS = ("sync", "gpsimd", "vector", "scalar", "tensor")


class Sem:
    __slots__ = ("h", "val", "name")

    def __init__(self, h, name):
        self.h = h
        self.val = 0
        self.name = name


def build_program(dbg=False, ntiles=NTILES, upto=None):
    nc = bass.Bass("TRN2", target_bir_lowering=False,
                   detect_race_conditions=False)

    def din_t(name, shape, dt):
        return nc.dram_tensor(name, shape, dt, kind="ExternalInput").ap()

    i_xT = din_t("i_xT", [HID, SEQ], F32)
    i_qwg = din_t("i_qwg", [FFI, 512], I32)
    i_qwu = din_t("i_qwu", [FFI, 512], I32)
    i_qwd = din_t("i_qwd", [HID, 1376], I32)
    i_sg = din_t("i_sg", [32, FFI], F32)
    i_su = din_t("i_su", [32, FFI], F32)
    i_sd = din_t("i_sd", [86, HID], F32)
    i_zg = din_t("i_zg", [32, 1376], I32)
    i_zu = din_t("i_zu", [32, 1376], I32)
    i_zd = din_t("i_zd", [86, 512], I32)
    i_winT = din_t("i_winT", [HID, MIN], F32)
    i_woutT = din_t("i_woutT", [IM, HID], F32)
    i_convw = din_t("i_convw", [128, CVC * 4], F32)
    i_convb = din_t("i_convb", [128, CVC], F32)
    i_nw = din_t("i_nw", [128, CCN], F32)
    i_D = din_t("i_D", [128, CCN], F32)
    i_dtb = din_t("i_dtb", [128, 1], F32)
    i_alog = din_t("i_alog", [128, 1], F32)
    i_idb = din_t("i_idb", [128, 128], BF16)
    i_idf = din_t("i_idf", [128, 128], F32)
    i_onesb = din_t("i_onesb", [1, 128], BF16)
    i_onesf = din_t("i_onesf", [1, 128], F32)
    i_onescol = din_t("i_onescol", [128, 1], BF16)
    i_ut = din_t("i_ut", [128, 128], BF16)
    i_eps = din_t("i_eps", [128, 1], F32)

    o_xT = nc.dram_tensor("o_xT", [HID, SEQ], F32, kind="ExternalOutput").ap()

    def scr(name, shape, dt, dbgout=False):
        if dbgout and dbg:
            return nc.dram_tensor(name, shape, dt, kind="ExternalOutput").ap()
        return nc.dram_tensor(name, shape, dt).ap()

    sX = scr("sX", [HID, SEQ], F32, dbgout=True)
    sWg = scr("sWg", [FC, 128, KC * 128], BF16, dbgout=True)
    sWu = scr("sWu", [FC, 128, KC * 128], BF16)
    sWd = scr("sWd", [KC, 128, FC * 128], BF16)
    sWin = scr("sWin", [MCI, 128, KC * 128], BF16, dbgout=True)
    sWout = scr("sWout", [KC, 128, CCN * 128], BF16)
    sGATE = scr("sGATE", [CCN, 128, TT], BF16, dbgout=True)
    sHS = scr("sHS", [CCN, 128, TT], BF16, dbgout=True)
    sAC = scr("sAC", [128, 128], F32)
    sST = scr("sST", [128, IM], F32, dbgout=True)

    with ExitStack() as ctx:
        def sb(name, shape, dt):
            return ctx.enter_context(nc.sbuf_tensor(name, shape, dt))

        c_idb = sb("c_idb", [128, 128], BF16)
        c_idf = sb("c_idf", [128, 128], F32)
        c_onesb = sb("c_onesb", [1, 128], BF16)
        c_onesf = sb("c_onesf", [1, 128], F32)
        c_onescol = sb("c_onescol", [128, 1], BF16)
        c_ut = sb("c_ut", [128, 128], BF16)
        c_convw = sb("c_convw", [128, CVC * 4], F32)
        c_convb = sb("c_convb", [128, CVC], F32)
        c_nw = sb("c_nw", [128, CCN], F32)
        c_D = sb("c_D", [128, CCN], F32)
        c_dtb = sb("c_dtb", [128, 1], F32)
        c_eA = sb("c_eA", [128, 1], F32)
        c_eps = sb("c_eps", [128, 1], F32)

        xn = sb("xn", [128, KC * TT], BF16)
        har = sb("har", [128, FC * TT], BF16)
        war = sb("war", [128, 16512], BF16)
        bt_t = sb("bt_t", [128, NG * TT], BF16)
        ct_t = sb("ct_t", [128, NG * TT], BF16)
        rows = sb("rows", [128, 2240], F32)
        tails_t = sb("tails", [128, 480], F32)
        xsl = [sb(f"xsl{i}", [128, TT], F32) for i in range(2)]
        s52 = [sb(f"s52_{i}", [128, 512], F32) for i in range(2)]
        xbf = sb("xbf", [128, TT], BF16)
        sqbf = sb("sqbf", [128, TT], BF16)
        dt_f = sb("dt_f", [128, TT], F32)
        adt = sb("adt", [128, TT], F32)
        sst = sb("sst", [128, 512], F32)
        cbt = sb("cbt", [128, 128], BF16)
        b_lg = sb("b_lg", [128, 128], BF16)
        stbf = [sb(f"stbf{i}", [128, 256], BF16) for i in range(2)]
        st_sl = [sb(f"st_sl{i}", [128, 1024], F32) for i in range(2)]
        g2h = [sb(f"g2h{i}", [128, 128], BF16) for i in range(2)]

        acum = sst[:, 0:128]
        acumT = sst[:, 128:256]
        decT = sst[:, 256:384]
        dtT = sst[:, 384:512]

        rows_bf = rows.ap().bitcast(BF16)
        rowA = rows[0:1, 0:512]
        rowB = rows[0:1, 512:1024]
        rows_abf = rows_bf[0:1, 2048:2560]
        rows_bbf = rows_bf[0:1, 2560:3072]
        facr = rows[0:1, 0:2048]
        rowC = rows[0:1, 2048:2176]
        rstd_bf = rows_bf[0:1, 4352:4480]
        tails = [tails_t[:, 240 * i: 240 * (i + 1)] for i in range(2)]

        war_f = war.ap().bitcast(F32)
        war_i = war.ap().bitcast(I32)
        har_f = har.ap().bitcast(F32)
        har_i = har.ap().bitcast(I32)
        tmp128 = har_f[:, 20480:20608]
        convin = har_f[:, 20608:21128]
        cacc = har_f[:, 21128:21640]
        eab = har_f[:, 21640:21768]
        coldec = har_f[:, 21768:21896]

        def mksem(name):
            return Sem(ctx.enter_context(nc.semaphore(name)), name)

        sGC, sPE, sACT, sDVE = (mksem(n) for n in ("sGC", "sPE", "sACT", "sDVE"))
        DMASEMS = []

        def mkdsem(name):
            s = mksem(name)
            DMASEMS.append(s)
            return s

        dK = mkdsem("dK")
        dWIN = mkdsem("dWIN")
        dF = [mkdsem(f"dF{i}") for i in range(2)]
        dO = [mkdsem(f"dO{i}") for i in range(2)]
        dQ = mkdsem("dQ")
        dSO = mkdsem("dSO")
        dXS = [mkdsem(f"dXS{i}") for i in range(2)]
        dXW = [mkdsem(f"dXW{i}") for i in range(2)]
        dWG = [mkdsem(f"dWG{i}") for i in range(2)]
        dWU = [mkdsem(f"dWU{i}") for i in range(2)]
        dWD = [mkdsem(f"dWD{i}") for i in range(3)]
        dWI = [mkdsem(f"dWI{i}") for i in range(3)]
        dWO = [mkdsem(f"dWO{i}") for i in range(2)]
        dGS = [mkdsem(f"dGS{i}") for i in range(2)]
        dHS = mkdsem("dHS")
        dSG = mkdsem("dSG")
        dSSr = [mkdsem(f"dSSr{i}") for i in range(2)]
        dSSw = [mkdsem(f"dSSw{i}") for i in range(2)]
        dFA = mkdsem("dFA")
        dAC = mkdsem("dAC")
        ALLSEMS = [sGC, sPE, sACT, sDVE] + DMASEMS

        sched = {e: [] for e in ENGS}
        wait_hi = {}

        def W(eng, sem, val=None):
            v = sem.val if val is None else val
            if v <= 0:
                return
            assert v <= sem.val, f"wait>{sem.name} sched"
            key = (eng, sem.name)
            if wait_hi.get(key, -1) >= v:
                return
            wait_hi[key] = v
            sched[eng].append(lambda e, s=sem.h, vv=v: e.wait_ge(s, vv))

        SELFSEM = {"vector": sDVE, "scalar": sACT, "gpsimd": sGC}

        def I(eng, fn, sem=None, n=1):
            own = SELFSEM.get(eng)
            if own is not None and own.val > 0 and (sem is not None or eng != "gpsimd"):
                W(eng, own)
            if sem is None:
                sched[eng].append(fn)
            else:
                sem.val += n
                sched[eng].append(lambda e, f=fn, s=sem.h, nn=n: f(e).then_inc(s, nn))

        def dma(q, dst, src, sem):
            I(q, lambda e, d=dst, s=src: e.dma_start(d, s), sem, 16)

        def gate_all(eng):
            for s in ALLSEMS:
                W(eng, s)

        class Bank:
            def __init__(self, i):
                self.t = ctx.enter_context(nc.psum_tensor(f"ps{i}", [128, 1024], BF16))
                self.b = self.t[:, :]
                self.f = self.t.ap().bitcast(F32)
                self.readers = []

            def pre_write(self):
                for (s, v) in self.readers:
                    W("tensor", s, v)
                self.readers.clear()

            def rd(self, sem):
                self.readers.append((sem, sem.val))

        B = [Bank(i) for i in range(8)]

        def mm_accum(bank, lhsT_slices, rhs_slices):
            n = len(lhsT_slices)
            for k in range(n):
                last = (k == n - 1)
                I("tensor", lambda e, o=bank.f[:, :], l=lhsT_slices[k],
                  r=rhs_slices[k], st=(k == 0), sp=last:
                  e.matmul(o, l, r, start=st, stop=sp), sPE if last else None)

        for dst, src in [(c_idb, i_idb), (c_idf, i_idf), (c_onesb, i_onesb),
                         (c_onesf, i_onesf), (c_onescol, i_onescol), (c_ut, i_ut),
                         (c_convw, i_convw), (c_convb, i_convb), (c_nw, i_nw),
                         (c_D, i_D), (c_dtb, i_dtb), (c_eps, i_eps),
                         (c_eA, i_alog)]:
            dma("sync", dst[:, :], src, dK)
        W("scalar", dK)
        W("vector", dK)
        W("tensor", dK)
        W("gpsimd", dK)
        I("scalar", lambda e: e.activation(c_eA[:, :], c_eA[:, :], AF.Exp), sACT)
        I("gpsimd", lambda e: e.memset(tails[0][:, :], 0.0), sGC)
        I("gpsimd", lambda e: e.memset(tails[1][:, :], 0.0), sGC)

        for mc in range(MCI):
            src = i_winT[:, mc * 128:(mc + 1) * 128].rearrange(
                "(kc k) m -> k kc m", k=128)
            dst = sWin[mc].rearrange("k (kc m) -> k kc m", m=128)
            dma("gpsimd", dst, src, dWIN)

        woc_dve = {}
        woc_dmao = {}
        for cc in range(CCN):
            pb = cc % 2
            fbuf = har_f[:, pb * 4096: pb * 4096 + 4096]
            bbuf = har[:, 16384 + pb * 4096: 16384 + pb * 4096 + 4096]
            if cc >= 2:
                W("sync", sDVE, woc_dve[pb])
            dma("sync", fbuf, i_woutT[cc * 128:(cc + 1) * 128, :], dF[pb])
            W("vector", dF[pb])
            if cc >= 2:
                W("vector", dO[pb])
            I("vector", lambda e, o=bbuf, i_=fbuf, c=cc:
              e.tensor_scalar(o, i_, c_nw[:, c:c + 1], None, ALU.mult), sDVE)
            woc_dve[pb] = sDVE.val
            W("gpsimd", sDVE)
            dst2 = sWout.rearrange("mc c (kc m) -> c mc kc m", m=128)[:, :, cc, :]
            dma("gpsimd", dst2, bbuf.rearrange("c (mc m) -> c mc m", m=128), dO[pb])
        W("scalar", dO[0])
        W("scalar", dO[1])
        W("vector", dO[0])
        W("vector", dO[1])

        def emit_dequant(qwT, qz, s_nat, scratch_w, NCH, KCH):
            K8 = KCH * 16
            GP = KCH
            KB = KCH * 128
            nq = 4 if GP % 4 == 0 else 2
            qq = K8 // nq
            w4n = KB // nq
            stage = har[:, 0:KB]
            qoff = KB // 2
            qbuf = har_i[:, qoff:qoff + K8]
            woff = qoff * 2 + K8 * 2
            wbfT = har[:, woff:woff + KB]
            w4off = woff + KB
            w4q = har_i[:, w4off // 2: w4off // 2 + w4n]
            assert (w4off // 2 + w4n) * 4 <= FC * TT * 2, "har overflow"
            zsc = war_f[:, 0:GP]
            ssc = war_f[:, 128:128 + GP]
            zraw = war_i[:, 512:528]
            zint = war_i[:, 640:768]
            zf = war_f[:, 1024:1152]
            sfc = war_f[:, 1280:1408]

            tp_done = 0
            q_war_dve = 0
            q_war_pe = 0
            W("sync", sDVE)
            W("sync", sACT)
            W("sync", sPE)
            for ncb in range(NCH):
                if ncb >= 1:
                    W("sync", sDVE, q_war_dve)
                    W("sync", sPE, q_war_pe)
                dma("sync", zraw[0:GP, :], qz[:, ncb * 16:(ncb + 1) * 16], dQ)
                dma("sync", sfc[0:GP, :], s_nat[:, ncb * 128:(ncb + 1) * 128], dQ)
                dma("sync", qbuf[:, :], qwT[ncb * 128:(ncb + 1) * 128, :], dQ)
                W("vector", dQ)
                W("vector", sACT)
                for j in range(8):
                    I("vector", lambda e, o=zint[0:GP, j::8], i_=zraw[0:GP, :], jj=j:
                      e.tensor_scalar(o, i_, 4 * jj, 15, ALU.logical_shift_right,
                                      ALU.bitwise_and), sDVE)
                I("vector", lambda e: e.scalar_tensor_tensor(
                    zf[0:GP, :], zint[0:GP, :], 1.0, sfc[0:GP, :], ALU.add, ALU.mult),
                  sDVE)
                bz = B[6]
                bz.pre_write()
                W("tensor", sDVE)
                I("tensor", lambda e: e.transpose(bz.f[:, 0:GP], zf[0:GP, :],
                                                  c_idf[0:GP, 0:GP]), sPE)
                I("tensor", lambda e: e.transpose(bz.f[:, 128:128 + GP], sfc[0:GP, :],
                                                  c_idf[0:GP, 0:GP]), sPE)
                W("scalar", sPE)
                W("scalar", sDVE)
                I("scalar", lambda e: e.copy(zsc, bz.f[:, 0:GP]), sACT)
                I("scalar", lambda e: e.copy(ssc, bz.f[:, 128:128 + GP]), sACT)
                bz.rd(sACT)
                W("vector", dQ)
                W("vector", sACT)
                if ncb >= 1:
                    W("vector", sPE, tp_done)
                for q in range(nq):
                    for j in range(8):
                        I("vector", lambda e, o=w4q[:, j::8], qs=q * qq, jj=j:
                          e.tensor_scalar(o, qbuf[:, qs:qs + qq], 4 * jj, 15,
                                          ALU.logical_shift_right, ALU.bitwise_and),
                          sDVE)
                    g0 = q * (GP // nq)
                    for gl in range(GP // nq):
                        g = g0 + gl
                        I("vector", lambda e, gg=g, gloc=gl:
                          e.tensor_scalar(wbfT[:, gg * 128:(gg + 1) * 128],
                                          w4q[:, gloc * 128:(gloc + 1) * 128],
                                          ssc[:, gg:gg + 1], zsc[:, gg:gg + 1],
                                          ALU.mult, ALU.subtract), sDVE)
                q_war_dve = sDVE.val
                W("scalar", dSO)
                for g in range(GP):
                    bk = B[6 + (g % 2)]
                    bk.pre_write()
                    W("tensor", sDVE)
                    I("tensor", lambda e, gg=g, o=bk.b[:, 0:128]:
                      e.transpose(o, wbfT[:, gg * 128:(gg + 1) * 128], c_idb[:, :]),
                      sPE)
                    W("scalar", sPE)
                    I("scalar", lambda e, gg=g, i_=bk.b[:, 0:128]:
                      e.copy(stage[:, gg * 128:(gg + 1) * 128], i_), sACT)
                    bk.rd(sACT)
                tp_done = sPE.val
                q_war_pe = sPE.val
                W("gpsimd", sACT)
                dma("gpsimd", scratch_w[ncb], stage, dSO)

        emit_dequant(i_qwg, i_zg, i_sg, sWg, FC, KC)
        emit_dequant(i_qwu, i_zu, i_su, sWu, FC, KC)
        emit_dequant(i_qwd, i_zd, i_sd, sWd, KC, FC)

        for e in ENGS:
            gate_all(e)

        state = {"xsl_war": {}, "s52_war": {}}

        def ln_stats_pass(src):
            B[0].pre_write()
            B[1].pre_write()
            W("sync", dXW[0])
            W("sync", dXW[1])
            W("vector", sPE)
            W("scalar", sPE)
            xw = dict(state["xsl_war"])
            mmv = 0
            for kc in range(KC):
                pb = kc % 2
                if pb in xw:
                    W("sync", sDVE, xw[pb][0])
                    if xw[pb][1]:
                        W("sync", sACT, xw[pb][1])
                dma("sync", xsl[pb][:, :], src[kc * 128:(kc + 1) * 128, :], dXS[pb])
                W("vector", dXS[pb])
                if kc >= 1:
                    W("vector", sPE, mmv)
                I("vector", lambda e, p=pb: e.tensor_copy(xbf[:, :], xsl[p][:, :]),
                  sDVE)
                W("scalar", dXS[pb])
                if kc >= 1:
                    W("scalar", sPE, mmv)
                I("scalar", lambda e, p=pb: e.activation(
                    sqbf[:, :], xsl[p][:, :], AF.Square), sACT)
                xw[pb] = (sDVE.val, sACT.val)
                W("tensor", sDVE)
                W("tensor", sACT)
                I("tensor", lambda e, st=(kc == 0), sp=(kc == KC - 1):
                  e.matmul(B[0].f[0:1, :], c_onescol[:, :], xbf[:, :],
                           start=st, stop=sp), sPE)
                I("tensor", lambda e, st=(kc == 0), sp=(kc == KC - 1):
                  e.matmul(B[1].f[0:1, :], c_onescol[:, :], sqbf[:, :],
                           start=st, stop=sp), sPE)
                mmv = sPE.val
            state["xsl_war"] = xw

        def ln_rows_and_bcast():
            W("scalar", sPE)
            I("scalar", lambda e: e.activation(rowA, B[0].f[0:1, :], AF.Identity,
                                               scale=1.0 / HID), sACT)
            I("scalar", lambda e: e.activation(rowB, B[0].f[0:1, :], AF.Square,
                                               scale=1.0 / HID), sACT)
            B[0].rd(sACT)
            W("vector", sACT)
            I("vector", lambda e: e.scalar_tensor_tensor(
                rowB, B[1].f[0:1, :], 1.0 / HID, rowB, ALU.mult, ALU.subtract), sDVE)
            B[1].rd(sDVE)
            W("scalar", sDVE)
            I("scalar", lambda e: e.activation(rowB, rowB, AF.Ln, bias=c_eps[0:1, 0:1]), sACT)
            I("scalar", lambda e: e.activation(rowB, rowB, AF.Exp, scale=-0.5), sACT)
            I("scalar", lambda e: e.activation(rows_abf, rowB, AF.Identity), sACT)
            W("vector", sACT)
            I("vector", lambda e: e.scalar_tensor_tensor(
                rowA, rowA, -1.0, rowB, ALU.mult, ALU.mult), sDVE)
            W("scalar", sDVE)
            I("scalar", lambda e: e.activation(rows_bbf, rowA, AF.Identity), sACT)
            B[2].pre_write()
            B[3].pre_write()
            W("tensor", sACT)
            I("tensor", lambda e: e.matmul(B[2].f[:, :], c_onesb[:, :], rows_abf,
                                           start=True, stop=True), sPE)
            I("tensor", lambda e: e.matmul(B[3].f[:, :], c_onesb[:, :], rows_bbf,
                                           start=True, stop=True), sPE)

        def ln_build_xn(src, xn_free_pe):
            W("vector", sPE, xn_free_pe)
            W("sync", dXW[0])
            W("sync", dXW[1])
            xw = dict(state["xsl_war"])
            for kc in range(KC):
                pb = kc % 2
                if pb in xw:
                    W("sync", sDVE, xw[pb][0])
                dma("sync", xsl[pb][:, :], src[kc * 128:(kc + 1) * 128, :], dXS[pb])
                W("vector", dXS[pb])
                W("vector", sPE)
                I("vector", lambda e, p=pb: e.tensor_tensor(
                    s52[0][:, :], xsl[p][:, :], B[2].f[:, :], ALU.mult), sDVE)
                I("vector", lambda e, k=kc: e.tensor_tensor(
                    xn[:, k * TT:(k + 1) * TT], s52[0][:, :], B[3].f[:, :],
                    ALU.add), sDVE)
                xw[pb] = (sDVE.val, 0)
            B[2].rd(sDVE)
            B[3].rd(sDVE)
            state["xsl_war"] = xw

        def residual_evac(bank, mc, dst_sl, scale, fuse_stats, nst):
            pb = mc % 2
            W("vector", dXS[pb])
            W("vector", sPE)
            if scale == 1.0:
                I("vector", lambda e: e.tensor_tensor(
                    s52[pb][:, :], xsl[pb][:, :], bank.f[:, :], ALU.add), sDVE)
            else:
                I("vector", lambda e: e.scalar_tensor_tensor(
                    s52[pb][:, :], bank.f[:, :], scale, xsl[pb][:, :],
                    ALU.mult, ALU.add), sDVE)
            bank.rd(sDVE)
            W("gpsimd", sDVE)
            dma("gpsimd", dst_sl[mc * 128:(mc + 1) * 128, :], s52[pb][:, :], dXW[pb])
            if fuse_stats:
                if mc == 0:
                    B[0].pre_write()
                    B[1].pre_write()
                if mc >= 1:
                    W("vector", sPE, nst[0])
                I("vector", lambda e, p=pb: e.tensor_copy(xbf[:, :], s52[p][:, :]),
                  sDVE)
                if mc >= 1:
                    W("scalar", sPE, nst[0])
                W("scalar", sDVE)
                I("scalar", lambda e, p=pb: e.activation(
                    sqbf[:, :], s52[p][:, :], AF.Square), sACT)
                W("tensor", sDVE)
                W("tensor", sACT)
                I("tensor", lambda e, st=(mc == 0), sp=(mc == KC - 1):
                  e.matmul(B[0].f[0:1, :], c_onescol[:, :], xbf[:, :],
                           start=st, stop=sp), sPE)
                I("tensor", lambda e, st=(mc == 0), sp=(mc == KC - 1):
                  e.matmul(B[1].f[0:1, :], c_onescol[:, :], sqbf[:, :],
                           start=st, stop=sp), sPE)
                nst[0] = sPE.val
            return (sDVE.val, 0)

        def emit_ffn(t0, src, dst_sl, fuse_stats):
            gate_all("sync")
            guw = {}
            for fc in range(FC):
                pb = fc % 2
                gB, uB = B[4 + pb], B[6 + pb]
                gsl = war[:, pb * 4096: pb * 4096 + 4096]
                usl = war[:, 8192 + pb * 4096: 8192 + pb * 4096 + 4096]
                if pb in guw:
                    W("sync", sPE, guw[pb])
                dma("sync", gsl, sWg[fc], dWG[pb])
                dma("sync", usl, sWu[fc], dWU[pb])
                gB.pre_write()
                uB.pre_write()
                W("tensor", dWG[pb])
                W("tensor", dWU[pb])
                mm_accum(gB, [gsl[:, k * 128:(k + 1) * 128] for k in range(KC)],
                         [xn[:, k * TT:(k + 1) * TT] for k in range(KC)])
                mm_accum(uB, [usl[:, k * 128:(k + 1) * 128] for k in range(KC)],
                         [xn[:, k * TT:(k + 1) * TT] for k in range(KC)])
                guw[pb] = sPE.val
                hsl = har[:, fc * TT:(fc + 1) * TT]
                W("scalar", sPE)
                I("scalar", lambda e, o=hsl, b_=gB: e.activation(o, b_.f[:, :],
                                                                 AF.Silu), sACT)
                gB.rd(sACT)
                W("vector", sACT)
                I("vector", lambda e, o=hsl, b_=uB: e.tensor_tensor(
                    o, o, b_.f[:, :], ALU.mult), sDVE)
                uB.rd(sDVE)
            h_done = sDVE.val
            dsw = {}
            nst = [0]
            QW = 2752
            dsw = {}
            for mc in range(KC):
                pb = mc % 2
                dB = B[4 + pb]
                bufs = []
                for half in range(2):
                    th = (2 * mc + half) % 3
                    dsl = war[:, th * 5504: th * 5504 + 5504]
                    bufs.append(dsl)
                    if th in dsw:
                        W("sync", sPE, dsw[th])
                    elif mc == 0 and half == 0:
                        W("sync", sPE, guw[0])
                        W("sync", sPE, guw[1])
                    dma("sync", dsl[:, 0:2752], sWd[mc][:, half * 5504:half * 5504 + 2752], dWD[th])
                    dma("sync", dsl[:, 2752:5504],
                        sWd[mc][:, half * 5504 + 2752:half * 5504 + 5504], dWD[th])
                xwp = state["xsl_war"].get(pb)
                if xwp is not None:
                    W("sync", sDVE, xwp[0])
                dma("sync", xsl[pb][:, :], src[mc * 128:(mc + 1) * 128, t0:t0 + TT],
                    dXS[pb])
                dB.pre_write()
                W("tensor", dWD[(2 * mc) % 3])
                W("tensor", dWD[(2 * mc + 1) % 3])
                W("tensor", sDVE, h_done)
                lhs, rhs = [], []
                for k in range(FC):
                    half = k // 43
                    kk = k % 43
                    lhs.append(bufs[half][:, kk * 128:(kk + 1) * 128])
                    rhs.append(har[:, k * TT:(k + 1) * TT])
                mm_accum(dB, lhs, rhs)
                dsw[(2 * mc) % 3] = sPE.val
                dsw[(2 * mc + 1) % 3] = sPE.val
                v = residual_evac(dB, mc, dst_sl, 0.5, fuse_stats, nst)
                state["xsl_war"][pb] = v

        def emit_inproj(t0, ssm_prev_loads):
            gate_all("sync")
            W("gpsimd", dHS)
            W("gpsimd", dSG)
            ipw = {}
            gw_war = {}
            for mc in range(MCI):
                pb = mc % 2
                th = mc % 3
                pB = B[4 + pb]
                wsl = war[:, th * 4096: th * 4096 + 4096]
                if th in ipw:
                    W("sync", sPE, ipw[th])
                dma("sync", wsl, sWin[mc], dWI[th])
                pB.pre_write()
                W("tensor", dWI[th])
                mm_accum(pB, [wsl[:, k * 128:(k + 1) * 128] for k in range(KC)],
                         [xn[:, k * TT:(k + 1) * TT] for k in range(KC)])
                ipw[th] = sPE.val
                if mc < CCN:
                    stg = xbf if (mc % 2 == 0) else sqbf
                    if mc % 2 in gw_war:
                        W("scalar", dGS[mc % 2])
                    W("scalar", sPE)
                    I("scalar", lambda e, o=stg[:, :], b_=pB:
                      e.activation(o, b_.f[:, :], AF.Silu), sACT)
                    pB.rd(sACT)
                    W("gpsimd", sACT)
                    dma("gpsimd", sGATE[mc], stg[:, :], dGS[mc % 2])
                    gw_war[mc % 2] = 1
                elif mc < CCN + CVC:
                    cc = mc - CCN
                    tl_r = tails[(t0 // TT) % 2]
                    tl_w = tails[((t0 // TT) + 1) % 2]
                    W("vector", sPE)
                    I("vector", lambda e, b_=pB: e.tensor_copy(convin[:, 3:515],
                                                               b_.f[:, :]), sDVE)
                    I("vector", lambda e, c=cc, t_=tl_r: e.tensor_copy(
                        convin[:, 0:3], t_[:, c * 3:(c + 1) * 3]), sDVE)
                    I("vector", lambda e, c=cc, t_=tl_w, b_=pB: e.tensor_copy(
                        t_[:, c * 3:(c + 1) * 3], b_.f[:, 509:512]), sDVE)
                    pB.rd(sDVE)
                    I("vector", lambda e, c=cc: e.tensor_scalar(
                        cacc[:, :], convin[:, 0:512],
                        c_convw[:, c * 4:c * 4 + 1], None, ALU.mult), sDVE)
                    for j in range(1, 4):
                        I("vector", lambda e, c=cc, jj=j: e.scalar_tensor_tensor(
                            cacc[:, :], convin[:, jj:512 + jj],
                            c_convw[:, c * 4 + jj:c * 4 + jj + 1], cacc[:, :],
                            ALU.mult, ALU.add), sDVE)
                    W("scalar", sDVE)
                    if cc < CCN:
                        stg = xbf if (mc % 2 == 0) else sqbf
                        if mc % 2 in gw_war:
                            W("scalar", dGS[mc % 2])
                        I("scalar", lambda e, c=cc, o=stg[:, :]: e.activation(
                            o, cacc[:, :], AF.Silu, bias=c_convb[:, c:c + 1]),
                          sACT)
                        W("gpsimd", sACT)
                        dma("gpsimd", sHS[cc], stg[:, :], dGS[mc % 2])
                        gw_war[mc % 2] = 1
                    else:
                        gidx = cc - CCN
                        dstt = bt_t if gidx < NG else ct_t
                        gg = gidx % NG
                        I("scalar", lambda e, c=cc, d_=dstt, g_=gg: e.activation(
                            d_[:, g_ * TT:(g_ + 1) * TT], cacc[:, :], AF.Silu,
                            bias=c_convb[:, c:c + 1]), sACT)
                else:
                    dsc = s52[0][:, :]
                    W("scalar", sPE)
                    W("scalar", sDVE)
                    I("scalar", lambda e, b_=pB: e.activation(
                        dsc, b_.f[:, :], AF.Abs, bias=c_dtb[:, 0:1]), sACT)
                    I("scalar", lambda e: e.activation(dsc, dsc, AF.Exp, scale=-1.0),
                      sACT)
                    I("scalar", lambda e: e.activation(dsc, dsc, AF.Ln, bias=1.0),
                      sACT)
                    I("scalar", lambda e, b_=pB: e.activation(
                        adt[:, :], b_.f[:, :], AF.Relu, bias=c_dtb[:, 0:1]), sACT)
                    pB.rd(sACT)
                    W("vector", sACT)
                    I("vector", lambda e: e.tensor_tensor(dt_f[:, :], dsc, adt[:, :],
                                                          ALU.add), sDVE)
                    I("vector", lambda e: e.tensor_scalar(dt_f[:, :], dt_f[:, :],
                                                          100.0, None, ALU.min), sDVE)
                    I("vector", lambda e: e.tensor_scalar(
                        adt[:, :], dt_f[:, :], c_eA[:, 0:1], -1.0, ALU.mult, ALU.mult),
                      sDVE)
            return sPE.val

        def emit_ssm(t0, st_out):
            hs_ch = har[:, 32768:40960]
            xch = xn[:, 0:8192]
            sg_ch = xn[:, 8192:16384]
            ysb = war[:, 0:8192]
            ltf = [war_f[:, 4096:4608], war_f[:, 4608:5120]]
            lte = [war[:, 10240:10752], war[:, 10752:11264]]
            mtb = [war[:, 11264:11776], war[:, 11776:12288]]
            ebc = [war_f[:, 6144:6656], war_f[:, 6656:7168]]
            cew = [war[:, 14336:14848], war[:, 14848:15360]]

            gate_all("vector")
            gate_all("scalar")
            gate_all("tensor")
            st_war = {}
            facr_rd = 0
            loads_hi = 0
            for z in range(CPT):
                zg = (t0 // TT) * CPT + z
                zsl = slice(z * CH, (z + 1) * CH)
                for e in ("vector", "scalar", "tensor"):
                    W(e, sDVE)
                    W(e, sACT)
                    W(e, sPE)
                I("vector", lambda e, zs=zsl: e.tensor_tensor_scan(
                    acum, adt[:, zs], adt[:, zs], 0.0, ALU.add, ALU.bypass), sDVE)
                bk = B[6]
                bk.pre_write()
                W("tensor", sDVE)
                I("tensor", lambda e: e.transpose(bk.f[:, 0:128], acum, c_idf[:, :]),
                  sPE)
                W("scalar", sPE)
                I("scalar", lambda e: e.copy(acumT, bk.f[:, 0:128]), sACT)
                bk.rd(sACT)
                I("vector", lambda e: e.tensor_scalar(
                    tmp128, acum, acum[:, 127:128], None, ALU.subtract), sDVE)
                W("scalar", sDVE)
                I("scalar", lambda e: e.activation(tmp128, tmp128, AF.Exp, scale=-1.0),
                  sACT)
                bk2 = B[7]
                bk2.pre_write()
                W("tensor", sACT)
                I("tensor", lambda e: e.transpose(bk2.f[:, 0:128], tmp128,
                                                  c_idf[:, :]), sPE)
                W("scalar", sPE)
                I("scalar", lambda e: e.copy(decT, bk2.f[:, 0:128]), sACT)
                bk2.rd(sACT)
                bk.pre_write()
                I("tensor", lambda e, zs=zsl: e.transpose(bk.f[:, 0:128],
                                                          dt_f[:, zs], c_idf[:, :]),
                  sPE)
                W("scalar", sPE)
                I("scalar", lambda e: e.copy(dtT, bk.f[:, 0:128]), sACT)
                bk.rd(sACT)
                W("vector", sACT)
                I("vector", lambda e: e.tensor_tensor(coldec[:, :], dtT, decT,
                                                      ALU.mult), sDVE)
                bk2.pre_write()
                I("tensor", lambda e: e.transpose(bk2.f[0:1, 0:128], acum[:, 127:128],
                                                  c_idf[:, :]), sPE)
                W("scalar", sPE)
                I("scalar", lambda e: e.copy(rowC[0:1, 0:128], bk2.f[0:1, 0:128]),
                  sACT)
                bk2.rd(sACT)
                bk2.pre_write()
                W("tensor", sACT)
                I("tensor", lambda e: e.matmul(bk2.f[:, 0:128], c_onesf[:, :],
                                               rowC[0:1, 0:128], start=True,
                                               stop=True), sPE)
                W("scalar", sPE)
                I("scalar", lambda e: e.activation(eab[:, :], bk2.f[:, 0:128],
                                                   AF.Exp), sACT)
                bk2.rd(sACT)
                W("gpsimd", sDVE)
                W("gpsimd", dFA)
                dma("gpsimd", sAC[:, :], acum, dAC)
                W("sync", dGS[0])
                W("sync", dGS[1])
                W("sync", sDVE)
                W("sync", sACT)
                W("sync", sPE)
                dma("sync", hs_ch.rearrange("p (cc l) -> p cc l", l=CH),
                    sHS[:, :, zsl].rearrange("cc p l -> p cc l"), dHS)
                dma("sync", sg_ch.rearrange("p (cc l) -> p cc l", l=CH),
                    sGATE[:, :, zsl].rearrange("cc p l -> p cc l"), dSG)
                W("tensor", dHS)
                for cc in range(CCN):
                    bb = B[6 + cc % 2]
                    bb.pre_write()
                    I("tensor", lambda e, c=cc, o=bb.b[:, 0:128]:
                      e.transpose(o, hs_ch[:, c * 128:(c + 1) * 128], c_idb[:, :]),
                      sPE)
                    if cc % 2 == 0:
                        W("scalar", sPE)
                        I("scalar", lambda e, c=cc, i_=bb.b[:, 0:128]:
                          e.copy(xch[:, c * 128:(c + 1) * 128], i_), sACT)
                        bb.rd(sACT)
                    else:
                        W("vector", sPE)
                        I("vector", lambda e, c=cc, i_=bb.b[:, 0:128]:
                          e.tensor_copy(xch[:, c * 128:(c + 1) * 128], i_), sDVE)
                        bb.rd(sDVE)
                xch_v, xch_a = sDVE.val, sACT.val
                qt_war = {}
                for g in range(NG):
                    ssl = st_sl[g % 2]
                    if g % 2 in st_war:
                        W("sync", dSSw[g % 2])
                        W("sync", sDVE, st_war[g % 2])
                    if zg == 0:
                        W("gpsimd", sDVE, st_war.get(g % 2, 0))
                        W("gpsimd", dSSw[g % 2])
                        I("gpsimd", lambda e, s=ssl: e.memset(s[:, :], 0.0), sGC)
                        st_ld = (sGC, sGC.val)
                    else:
                        dma("sync", ssl[:, :], sST[:, g * 1024:(g + 1) * 1024],
                            dSSr[g % 2])
                        st_ld = (dSSr[g % 2], dSSr[g % 2].val)
                    W("gpsimd", dAC)
                    W("gpsimd", sPE, facr_rd)
                    facsrc = sAC.rearrange("p l -> (p l)").unsqueeze(0)
                    dma("gpsimd", facr, facsrc[:, g * 2048:(g + 1) * 2048], dFA)
                    bb = B[6 + g % 2]
                    bb.pre_write()
                    I("tensor", lambda e, o=bb.f[:, 0:128],
                      l_=bt_t[:, g * TT + z * CH: g * TT + z * CH + CH],
                      r_=ct_t[:, g * TT + z * CH: g * TT + z * CH + CH]:
                      e.matmul(o, l_, r_, start=True, stop=True), sPE)
                    W("vector", sPE)
                    I("vector", lambda e, b_=bb: e.tensor_tensor(
                        cbt[:, :], b_.f[:, 0:128], c_ut[:, :], ALU.mult), sDVE)
                    bb.rd(sDVE)
                    for q in range(4):
                        pq = q % 2
                        qb = B[2 + pq]
                        qb.pre_write()
                        W("tensor", dFA)
                        I("tensor", lambda e, qq=q, o=qb.f[:, :]:
                          e.matmul(o, c_onesf[:, :], facr[:, qq * 512:(qq + 1) * 512],
                                   start=True, stop=True), sPE)
                        h0 = g * 16 + q * 4
                        W("vector", sPE)
                        I("vector", lambda e, o=ltf[pq], b_=qb, hh=h0:
                          e.tensor_tensor(
                              o.rearrange("p (a b) -> p a b", a=4),
                              b_.f[:, :].rearrange("p (a b) -> p a b", a=4),
                              acumT[:, hh:hh + 4].unsqueeze(2).broadcast_to(
                                  [128, 4, 128]), ALU.subtract), sDVE)
                        I("vector", lambda e, o=ltf[pq]: e.tensor_scalar(
                            o, o, 0.0, None, ALU.min), sDVE)
                        W("scalar", sPE)
                        W("scalar", sDVE)
                        I("scalar", lambda e, o=ebc[pq], b_=qb:
                          e.activation(o, b_.f[:, :], AF.Exp), sACT)
                        qb.rd(sACT)
                        qb.rd(sDVE)
                        W("scalar", sDVE)
                        I("scalar", lambda e, o=lte[pq], i_=ltf[pq]:
                          e.activation(o, i_, AF.Exp), sACT)
                        W("vector", sACT)
                        for hh in range(4):
                            I("vector", lambda e, o=mtb[pq], i_=lte[pq], h=h0 + hh,
                              hl=hh: e.scalar_tensor_tensor(
                                  o[:, hl * 128:(hl + 1) * 128],
                                  i_[:, hl * 128:(hl + 1) * 128],
                                  dtT[:, h:h + 1], cbt[:, :], ALU.mult, ALU.mult),
                              sDVE)
                        I("vector", lambda e, o=cew[pq], eb_=ebc[pq],
                          ct_=ct_t[:, g * TT + z * CH: g * TT + z * CH + CH]
                          .unsqueeze(1).broadcast_to([128, 4, 128]):
                          e.tensor_tensor(
                              o.rearrange("p (a b) -> p a b", a=4), ct_,
                              eb_.rearrange("p (a b) -> p a b", a=4), ALU.mult), sDVE)
                        if zg > 0:
                            W("vector", st_ld[0], st_ld[1])
                            if pq in qt_war:
                                W("vector", sPE, qt_war[pq])
                            I("vector", lambda e, o=stbf[pq], s=ssl, qq=q:
                              e.tensor_copy(o[:, :], s[:, qq * 256:(qq + 1) * 256]),
                              sDVE)
                        yB = B[4 + (q // 2)]
                        if q == 0 or q == 2:
                            yB.pre_write()
                        W("tensor", sDVE)
                        for hh in range(4):
                            hglob = h0 + hh
                            off = ((hglob % 8)) * 64
                            I("tensor", lambda e, o=yB.f[:, :], m=mtb[pq], hl=hh,
                              hg=hglob, of=off, sp=(zg == 0):
                              e.matmul(o[:, of:of + 64],
                                       m[:, hl * 128:(hl + 1) * 128],
                                       xch[:, hg * 64:(hg + 1) * 64], start=True,
                                       stop=sp), sPE if zg == 0 else None)
                            if zg > 0:
                                I("tensor", lambda e, o=yB.f[:, :], c=cew[pq], hl=hh,
                                  sb_=stbf[pq], of=off:
                                  e.matmul(o[:, of:of + 64],
                                           c[:, hl * 128:(hl + 1) * 128],
                                           sb_[:, hl * 64:(hl + 1) * 64],
                                           start=False, stop=True), sPE)
                        qt_war[pq] = sPE.val
                        if q % 2 == 1:
                            W("scalar", sPE)
                            I("scalar", lambda e, gg=g, qh=q // 2, b_=yB:
                              e.copy(ysb[:, gg * 1024 + qh * 512:
                                         gg * 1024 + qh * 512 + 512], b_.f[:, :]),
                              sACT)
                            yB.rd(sACT)
                    facr_rd = sPE.val
                    W("vector", sDVE, xch_v)
                    W("vector", sACT, xch_a)
                    W("vector", sPE)
                    for hh in range(16):
                        h = g * 16 + hh
                        I("vector", lambda e, h_=h: e.tensor_scalar(
                            xch[:, h_ * 64:(h_ + 1) * 64],
                            xch[:, h_ * 64:(h_ + 1) * 64],
                            coldec[:, h_:h_ + 1], None, ALU.mult), sDVE)
                    bb = B[6 + g % 2]
                    bb.pre_write()
                    W("tensor", sDVE)
                    I("tensor", lambda e, o=bb.b[:, 0:128],
                      l_=bt_t[:, g * TT + z * CH: g * TT + z * CH + CH]:
                      e.transpose(o, l_, c_idb[:, :]), sPE)
                    W("vector", sPE)
                    I("vector", lambda e, b_=bb: e.tensor_copy(b_lg[:, :],
                                                               b_.b[:, 0:128]), sDVE)
                    bb.rd(sDVE)
                    for half in range(2):
                        sb_ = B[6 + (g + 1) % 2]
                        sb_.pre_write()
                        W("tensor", sDVE)
                        I("tensor", lambda e, o=sb_.f[:, :], gg=g, hf=half:
                          e.matmul(o, b_lg[:, :],
                                   xch[:, (gg * 16 + hf * 8) * 64:
                                       (gg * 16 + hf * 8 + 8) * 64],
                                   start=True, stop=True), sPE)
                        W("vector", st_ld[0], st_ld[1])
                        W("vector", sPE)
                        for hh in range(8):
                            h = g * 16 + half * 8 + hh
                            I("vector", lambda e, h_=h, b2=sb_, hl=hh, s=ssl,
                              hf=half: e.scalar_tensor_tensor(
                                  s[:, (hf * 8 + hl) * 64:(hf * 8 + hl + 1) * 64],
                                  s[:, (hf * 8 + hl) * 64:(hf * 8 + hl + 1) * 64],
                                  eab[:, h_:h_ + 1],
                                  b2.f[:, hl * 64:(hl + 1) * 64],
                                  ALU.mult, ALU.add), sDVE)
                        sb_.rd(sDVE)
                    W("gpsimd", sDVE)
                    dma("gpsimd", sST[:, g * 1024:(g + 1) * 1024], ssl[:, :],
                        dSSw[g % 2])
                    st_war[g % 2] = sDVE.val
                    W("vector", dHS)
                    W("vector", dSG)
                    for hh2 in range(8):
                        cc = g * 8 + hh2
                        bb = B[6 + hh2 % 2]
                        bb.pre_write()
                        W("tensor", sACT)
                        I("tensor", lambda e, c=cc, o=bb.b[:, 0:128]:
                          e.transpose(o, ysb[:, c * 128:(c + 1) * 128], c_idb[:, :]),
                          sPE)
                        W("vector", sPE)
                        gt = g2h[hh2 % 2]
                        I("vector", lambda e, c=cc, o=gt[:, :], b_=bb:
                          e.scalar_tensor_tensor(o, hs_ch[:, c * 128:(c + 1) * 128],
                                                 c_D[:, c:c + 1], b_.b[:, 0:128],
                                                 ALU.mult, ALU.add), sDVE)
                        bb.rd(sDVE)
                        I("vector", lambda e, c=cc, g_=gt:
                          e.tensor_tensor(sg_ch[:, c * 128:(c + 1) * 128],
                                          sg_ch[:, c * 128:(c + 1) * 128],
                                          g_[:, :], ALU.mult), sDVE)
                    W("scalar", sDVE)
                    I("scalar", lambda e, gg=g: e.activation(
                        ysb[:, gg * 1024:(gg + 1) * 1024],
                        sg_ch[:, gg * 1024:(gg + 1) * 1024], AF.Square), sACT)
                    qb = B[2 + g % 2]
                    qb.pre_write()
                    W("tensor", sACT)
                    for c8 in range(8):
                        I("tensor", lambda e, q_=qb, gg=g, c_=c8:
                          e.matmul(q_.f[0:1, 0:128], c_onescol[:, :],
                                   ysb[:, gg * 1024 + c_ * 128:
                                       gg * 1024 + c_ * 128 + 128],
                                   start=(c_ == 0), stop=(c_ == 7)),
                          sPE if c8 == 7 else None)
                    W("scalar", sPE)
                    I("scalar", lambda e, b_=qb: e.activation(
                        rowC[0:1, 0:128], b_.f[0:1, 0:128], AF.Ln,
                        bias=c_eps[0:1, 0:1], scale=1.0 / 1024.0), sACT)
                    qb.rd(sACT)
                    I("scalar", lambda e: e.activation(rstd_bf, rowC[0:1, 0:128],
                                                       AF.Exp, scale=-0.5), sACT)
                    rb = B[2 + (g + 1) % 2]
                    rb.pre_write()
                    W("tensor", sACT)
                    I("tensor", lambda e, o=rb.f[:, 0:128]:
                      e.matmul(o, c_onesb[:, :], rstd_bf, start=True, stop=True), sPE)
                    W("vector", sPE)
                    for c8 in range(8):
                        cc = g * 8 + c8
                        I("vector", lambda e, c=cc, b_=rb,
                          o=har[:, cc * TT + z * CH: cc * TT + z * CH + CH]:
                          e.tensor_tensor(
                              o, sg_ch[:, c * 128:(c + 1) * 128], b_.f[:, 0:128],
                              ALU.mult), sDVE)
                    rb.rd(sDVE)
            return 0

        def emit_outproj(t0, src, dst_sl):
            gate_all("sync")
            opw = {}
            nst = [0]
            for mc in range(KC):
                pb = mc % 2
                oB = B[4 + pb]
                wsl = war[:, pb * 8192: pb * 8192 + 8192]
                if pb in opw:
                    W("sync", sPE, opw[pb])
                dma("sync", wsl[:, 0:4096], sWout[mc][:, 0:4096], dWO[pb])
                dma("sync", wsl[:, 4096:8192], sWout[mc][:, 4096:8192], dWO[pb])
                xwp = state["xsl_war"].get(pb)
                if xwp is not None:
                    W("sync", sDVE, xwp[0])
                dma("sync", xsl[pb][:, :], src[mc * 128:(mc + 1) * 128, t0:t0 + TT],
                    dXS[pb])
                oB.pre_write()
                W("tensor", dWO[pb])
                mm_accum(oB, [wsl[:, k * 128:(k + 1) * 128] for k in range(CCN)],
                         [har[:, k * TT:(k + 1) * TT] for k in range(CCN)])
                opw[pb] = sPE.val
                v = residual_evac(oB, mc, dst_sl, 1.0, True, nst)
                state["xsl_war"][pb] = v

        st_out = {}
        ssm_loads = 0
        for t in range(ntiles):
            t0 = t * TT
            if upto == "prologue":
                break
            ln_stats_pass(i_xT[:, t0:t0 + TT])
            ln_rows_and_bcast()
            ln_build_xn(i_xT[:, t0:t0 + TT], sPE.val)
            emit_ffn(t0, i_xT, sX[:, t0:t0 + TT], fuse_stats=True)
            if upto == "ffn1":
                break
            ln_rows_and_bcast()
            ln_build_xn(sX[:, t0:t0 + TT], sPE.val)
            ip_done = emit_inproj(t0, ssm_loads)
            if upto == "inproj":
                break
            ssm_loads = emit_ssm(t0, st_out)
            if upto == "ssm":
                break
            emit_outproj(t0, sX, sX[:, t0:t0 + TT])
            if upto == "mamba":
                break
            ln_rows_and_bcast()
            ln_build_xn(sX[:, t0:t0 + TT], sPE.val)
            emit_ffn(t0, sX, o_xT[:, t0:t0 + TT], fuse_stats=False)

        for s_ in DMASEMS:
            W("sync", s_)

        with nc.Block() as block:
            def mk(name):
                def body(eng):
                    for fn in sched[name]:
                        fn(eng)
                return body
            block.sync(mk("sync"))
            block.gpsimd(mk("gpsimd"))
            block.vector(mk("vector"))
            block.scalar(mk("scalar"))
            block.tensor(mk("tensor"))

    return nc


_BUILT = {}


def _prep(inputs):
    x = np.asarray(inputs["L0_attn_x"])
    xT = np.ascontiguousarray(np.transpose(x, (0, 2, 1)))
    conv_w = np.asarray(inputs["L0_attn_conv_w"]).reshape(CONVD, 4)
    convw = np.ascontiguousarray(
        conv_w.reshape(CVC, 128, 4).transpose(1, 0, 2).reshape(128, CVC * 4))
    convb = np.ascontiguousarray(
        np.asarray(inputs["L0_attn_conv_b"]).reshape(CVC, 128).T)
    nw = np.ascontiguousarray(
        np.asarray(inputs["L0_attn_norm_w"]).reshape(CCN, 128).T)
    Dr = np.ascontiguousarray(
        np.repeat(np.asarray(inputs["L0_attn_D"]), PD).reshape(CCN, 128).T)
    shared = {
        "i_qwg": np.ascontiguousarray(np.asarray(inputs["L0_ffn_gate_qweight"]).T),
        "i_qwu": np.ascontiguousarray(np.asarray(inputs["L0_ffn_up_qweight"]).T),
        "i_qwd": np.ascontiguousarray(np.asarray(inputs["L0_ffn_down_qweight"]).T),
        "i_sg": np.ascontiguousarray(inputs["L0_ffn_gate_scales"]),
        "i_su": np.ascontiguousarray(inputs["L0_ffn_up_scales"]),
        "i_sd": np.ascontiguousarray(inputs["L0_ffn_down_scales"]),
        "i_zg": np.ascontiguousarray(inputs["L0_ffn_gate_qzeros"]),
        "i_zu": np.ascontiguousarray(inputs["L0_ffn_up_qzeros"]),
        "i_zd": np.ascontiguousarray(inputs["L0_ffn_down_qzeros"]),
        "i_winT": np.ascontiguousarray(np.asarray(inputs["L0_attn_in_proj_w"]).T),
        "i_woutT": np.ascontiguousarray(np.asarray(inputs["L0_attn_out_proj_w"]).T),
        "i_convw": convw.astype(np.float32),
        "i_convb": convb.astype(np.float32),
        "i_nw": nw.astype(np.float32),
        "i_D": Dr.astype(np.float32),
        "i_dtb": np.asarray(inputs["L0_attn_dt_bias"]).reshape(128, 1).astype(np.float32),
        "i_alog": np.asarray(inputs["L0_attn_A_log"]).reshape(128, 1).astype(np.float32),
        "i_idb": np.eye(128, dtype=ml_dtypes.bfloat16),
        "i_idf": np.eye(128, dtype=np.float32),
        "i_onesb": np.ones((1, 128), dtype=ml_dtypes.bfloat16),
        "i_onesf": np.ones((1, 128), dtype=np.float32),
        "i_onescol": np.ones((128, 1), dtype=ml_dtypes.bfloat16),
        "i_ut": np.triu(np.ones((128, 128), np.float32)).astype(ml_dtypes.bfloat16),
        "i_eps": np.full((128, 1), EPS, np.float32),
    }
    in_maps = []
    for c in range(NCORES):
        m = dict(shared)
        m["i_xT"] = np.ascontiguousarray(xT[c])
        in_maps.append(m)
    return in_maps


def kernel(**inputs):
    in_maps = _prep(inputs)
    if "nc" not in _BUILT:
        _BUILT["nc"] = build_program()
    nc = _BUILT["nc"]
    res = run_bass_kernel_spmd(nc, in_maps, core_ids=list(range(NCORES)))
    out = np.stack([np.asarray(r["o_xT"], dtype=np.float32).T for r in res.results])
    return np.ascontiguousarray(out)


# version 04
# speedup vs baseline: 1.0787x; 1.0787x over previous
import numpy as np
import ml_dtypes
import concourse.bass as bass
import concourse.mybir as mybir
from concourse.bass_utils import run_bass_kernel_spmd
from contextlib import ExitStack

F32 = mybir.dt.float32
BF16 = mybir.dt.bfloat16
I32 = mybir.dt.int32
AF = mybir.ActivationFunctionType
ALU = mybir.AluOpType

NCORES = 8
SEQ = 2048
HID = 4096
KC = HID // 128
TT = 512
NTILES = SEQ // TT
CH = 128
CPT = TT // CH
FFI = 11008
FC = FFI // 128
MIN = 18560
MCI = MIN // 128
IM = 8192
CCN = IM // 128
CONVD = 10240
CVC = CONVD // 128
NG = 8
PD = 64
EPS = 1e-5

ENGS = ("sync", "gpsimd", "vector", "scalar", "tensor")


class Sem:
    __slots__ = ("h", "val", "name")

    def __init__(self, h, name):
        self.h = h
        self.val = 0
        self.name = name


def build_program(dbg=False, ntiles=NTILES, upto=None):
    nc = bass.Bass("TRN2", target_bir_lowering=False,
                   detect_race_conditions=False)

    def din_t(name, shape, dt):
        return nc.dram_tensor(name, shape, dt, kind="ExternalInput").ap()

    i_xT = din_t("i_xT", [HID, SEQ], F32)
    i_qwg = din_t("i_qwg", [FFI, 512], I32)
    i_qwu = din_t("i_qwu", [FFI, 512], I32)
    i_qwd = din_t("i_qwd", [HID, 1376], I32)
    i_sg = din_t("i_sg", [32, FFI], F32)
    i_su = din_t("i_su", [32, FFI], F32)
    i_sd = din_t("i_sd", [86, HID], F32)
    i_zg = din_t("i_zg", [32, 1376], I32)
    i_zu = din_t("i_zu", [32, 1376], I32)
    i_zd = din_t("i_zd", [86, 512], I32)
    i_winT = din_t("i_winT", [HID, MIN], F32)
    i_woutT = din_t("i_woutT", [IM, HID], F32)
    i_convw = din_t("i_convw", [128, CVC * 4], F32)
    i_convb = din_t("i_convb", [128, CVC], F32)
    i_nw = din_t("i_nw", [128, CCN], F32)
    i_D = din_t("i_D", [128, CCN], F32)
    i_dtb = din_t("i_dtb", [128, 1], F32)
    i_alog = din_t("i_alog", [128, 1], F32)
    i_idb = din_t("i_idb", [128, 128], BF16)
    i_idf = din_t("i_idf", [128, 128], F32)
    i_onesb = din_t("i_onesb", [1, 128], BF16)
    i_onesf = din_t("i_onesf", [1, 128], F32)
    i_onescol = din_t("i_onescol", [128, 1], BF16)
    i_ut = din_t("i_ut", [128, 128], BF16)
    i_eps = din_t("i_eps", [128, 1], F32)

    o_xT = nc.dram_tensor("o_xT", [HID, SEQ], F32, kind="ExternalOutput").ap()

    def scr(name, shape, dt, dbgout=False):
        if dbgout and dbg:
            return nc.dram_tensor(name, shape, dt, kind="ExternalOutput").ap()
        return nc.dram_tensor(name, shape, dt).ap()

    sX = scr("sX", [HID, SEQ], F32, dbgout=True)
    sWg = scr("sWg", [FC, 128, KC * 128], BF16, dbgout=True)
    sWu = scr("sWu", [FC, 128, KC * 128], BF16)
    sWd = scr("sWd", [KC, 128, FC * 128], BF16)
    sWin = scr("sWin", [MCI, 128, KC * 128], BF16, dbgout=True)
    sWout = scr("sWout", [KC, 128, CCN * 128], BF16)
    sGATE = scr("sGATE", [CCN, 128, TT], BF16, dbgout=True)
    sHS = scr("sHS", [CCN, 128, TT], BF16, dbgout=True)
    sAC = scr("sAC", [128, 128], F32)
    sST = scr("sST", [128, IM], F32, dbgout=True)

    with ExitStack() as ctx:
        def sb(name, shape, dt):
            return ctx.enter_context(nc.sbuf_tensor(name, shape, dt))

        c_idb = sb("c_idb", [128, 128], BF16)
        c_idf = sb("c_idf", [128, 128], F32)
        c_onesb = sb("c_onesb", [1, 128], BF16)
        c_onesf = sb("c_onesf", [1, 128], F32)
        c_onescol = sb("c_onescol", [128, 1], BF16)
        c_ut = sb("c_ut", [128, 128], BF16)
        c_convw = sb("c_convw", [128, CVC * 4], F32)
        c_convb = sb("c_convb", [128, CVC], F32)
        c_nw = sb("c_nw", [128, CCN], F32)
        c_D = sb("c_D", [128, CCN], F32)
        c_dtb = sb("c_dtb", [128, 1], F32)
        c_eA = sb("c_eA", [128, 1], F32)
        c_eps = sb("c_eps", [128, 1], F32)

        xn = sb("xn", [128, KC * TT], BF16)
        har = sb("har", [128, FC * TT], BF16)
        war = sb("war", [128, 16512], BF16)
        bt_t = sb("bt_t", [128, NG * TT], BF16)
        ct_t = sb("ct_t", [128, NG * TT], BF16)
        rows = sb("rows", [128, 2240], F32)
        tails_t = sb("tails", [128, 480], F32)
        xsl = [sb(f"xsl{i}", [128, TT], F32) for i in range(2)]
        s52 = [sb(f"s52_{i}", [128, 512], F32) for i in range(2)]
        xbf = sb("xbf", [128, TT], BF16)
        sqbf = sb("sqbf", [128, TT], BF16)
        dt_f = sb("dt_f", [128, TT], F32)
        adt = sb("adt", [128, TT], F32)
        sst = sb("sst", [128, 512], F32)
        cbt = sb("cbt", [128, 128], BF16)
        b_lg = sb("b_lg", [128, 128], BF16)
        stbf = [sb(f"stbf{i}", [128, 256], BF16) for i in range(2)]
        st_sl = [sb(f"st_sl{i}", [128, 1024], F32) for i in range(2)]
        g2h = [sb(f"g2h{i}", [128, 128], BF16) for i in range(2)]

        acum = sst[:, 0:128]
        acumT = sst[:, 128:256]
        decT = sst[:, 256:384]
        dtT = sst[:, 384:512]

        rows_bf = rows.ap().bitcast(BF16)
        rowA = rows[0:1, 0:512]
        rowB = rows[0:1, 512:1024]
        rows_abf = rows_bf[0:1, 2048:2560]
        rows_bbf = rows_bf[0:1, 2560:3072]
        facr = rows[0:1, 0:2048]
        rowC = rows[0:1, 2048:2176]
        rstd_bf = rows_bf[0:1, 4352:4480]
        tails = [tails_t[:, 240 * i: 240 * (i + 1)] for i in range(2)]

        war_f = war.ap().bitcast(F32)
        war_i = war.ap().bitcast(I32)
        har_f = har.ap().bitcast(F32)
        har_i = har.ap().bitcast(I32)
        tmp128 = har_f[:, 20480:20608]
        convin = har_f[:, 20608:21128]
        cacc = har_f[:, 21128:21640]
        eab = har_f[:, 21640:21768]
        coldec = har_f[:, 21768:21896]

        def mksem(name):
            return Sem(ctx.enter_context(nc.semaphore(name)), name)

        sGC, sPE, sACT, sDVE = (mksem(n) for n in ("sGC", "sPE", "sACT", "sDVE"))
        DMASEMS = []

        def mkdsem(name):
            s = mksem(name)
            DMASEMS.append(s)
            return s

        dK = mkdsem("dK")
        dWIN = mkdsem("dWIN")
        dF = [mkdsem(f"dF{i}") for i in range(2)]
        dO = [mkdsem(f"dO{i}") for i in range(2)]
        dQ = mkdsem("dQ")
        dSO = mkdsem("dSO")
        dXS = [mkdsem(f"dXS{i}") for i in range(2)]
        dXW = [mkdsem(f"dXW{i}") for i in range(2)]
        dWG = [mkdsem(f"dWG{i}") for i in range(2)]
        dWU = [mkdsem(f"dWU{i}") for i in range(2)]
        dWD = [mkdsem(f"dWD{i}") for i in range(3)]
        dWI = [mkdsem(f"dWI{i}") for i in range(3)]
        dWO = [mkdsem(f"dWO{i}") for i in range(2)]
        dGS = [mkdsem(f"dGS{i}") for i in range(2)]
        dHS = mkdsem("dHS")
        dSG = mkdsem("dSG")
        dSSr = [mkdsem(f"dSSr{i}") for i in range(2)]
        dSSw = [mkdsem(f"dSSw{i}") for i in range(2)]
        dFA = mkdsem("dFA")
        dAC = mkdsem("dAC")
        ALLSEMS = [sGC, sPE, sACT, sDVE] + DMASEMS

        sched = {e: [] for e in ENGS}
        wait_hi = {}
        PH = ["init"]

        def set_phase(name):
            PH[0] = name

        def W(eng, sem, val=None):
            v = sem.val if val is None else val
            if v <= 0:
                return
            assert v <= sem.val, f"wait>{sem.name} sched"
            key = (eng, sem.name)
            if wait_hi.get(key, -1) >= v:
                return
            wait_hi[key] = v
            sched[eng].append((PH[0], lambda e, s=sem.h, vv=v: e.wait_ge(s, vv)))

        SELFSEM = {"vector": sDVE, "scalar": sACT, "gpsimd": sGC}

        def I(eng, fn, sem=None, n=1, ss=True):
            own = SELFSEM.get(eng)
            if ss and own is not None and own.val > 0 and (
                    sem is not None or eng != "gpsimd"):
                W(eng, own)
            if sem is None:
                sched[eng].append((PH[0], fn))
            else:
                sem.val += n
                sched[eng].append(
                    (PH[0], lambda e, f=fn, s=sem.h, nn=n: f(e).then_inc(s, nn)))

        def dma(q, dst, src, sem):
            I(q, lambda e, d=dst, s=src: e.dma_start(d, s), sem, 16)

        def gate_all(eng):
            for s in ALLSEMS:
                W(eng, s)

        class Bank:
            def __init__(self, i):
                self.t = ctx.enter_context(nc.psum_tensor(f"ps{i}", [128, 1024], BF16))
                self.b = self.t[:, :]
                self.f = self.t.ap().bitcast(F32)
                self.readers = []

            def pre_write(self):
                for (s, v) in self.readers:
                    W("tensor", s, v)
                self.readers.clear()

            def rd(self, sem):
                self.readers.append((sem, sem.val))

        B = [Bank(i) for i in range(8)]

        def mm_accum(bank, lhsT_slices, rhs_slices):
            n = len(lhsT_slices)
            for k in range(n):
                last = (k == n - 1)
                I("tensor", lambda e, o=bank.f[:, :], l=lhsT_slices[k],
                  r=rhs_slices[k], st=(k == 0), sp=last:
                  e.matmul(o, l, r, start=st, stop=sp), sPE if last else None)

        for dst, src in [(c_idb, i_idb), (c_idf, i_idf), (c_onesb, i_onesb),
                         (c_onesf, i_onesf), (c_onescol, i_onescol), (c_ut, i_ut),
                         (c_convw, i_convw), (c_convb, i_convb), (c_nw, i_nw),
                         (c_D, i_D), (c_dtb, i_dtb), (c_eps, i_eps),
                         (c_eA, i_alog)]:
            dma("sync", dst[:, :], src, dK)
        W("scalar", dK)
        W("vector", dK)
        W("tensor", dK)
        W("gpsimd", dK)
        I("scalar", lambda e: e.activation(c_eA[:, :], c_eA[:, :], AF.Exp), sACT)
        I("gpsimd", lambda e: e.memset(tails[0][:, :], 0.0), sGC)
        I("gpsimd", lambda e: e.memset(tails[1][:, :], 0.0), sGC)

        set_phase("p_conv")
        for mc in range(MCI):
            src = i_winT[:, mc * 128:(mc + 1) * 128].rearrange(
                "(kc k) m -> k kc m", k=128)
            dst = sWin[mc].rearrange("k (kc m) -> k kc m", m=128)
            dma("gpsimd", dst, src, dWIN)

        woc_dve = {}
        woc_dmao = {}
        for cc in range(CCN):
            pb = cc % 2
            fbuf = har_f[:, pb * 4096: pb * 4096 + 4096]
            bbuf = har[:, 16384 + pb * 4096: 16384 + pb * 4096 + 4096]
            if cc >= 2:
                W("sync", sDVE, woc_dve[pb])
            dma("sync", fbuf, i_woutT[cc * 128:(cc + 1) * 128, :], dF[pb])
            W("vector", dF[pb])
            if cc >= 2:
                W("vector", dO[pb])
            I("vector", lambda e, o=bbuf, i_=fbuf, c=cc:
              e.tensor_scalar(o, i_, c_nw[:, c:c + 1], None, ALU.mult), sDVE)
            woc_dve[pb] = sDVE.val
            W("gpsimd", sDVE)
            dst2 = sWout.rearrange("mc c (kc m) -> c mc kc m", m=128)[:, :, cc, :]
            dma("gpsimd", dst2, bbuf.rearrange("c (mc m) -> c mc m", m=128), dO[pb])
        W("scalar", dO[0])
        W("scalar", dO[1])
        W("vector", dO[0])
        W("vector", dO[1])

        def emit_dequant(qwT, qz, s_nat, scratch_w, NCH, KCH):
            K8 = KCH * 16
            GP = KCH
            KB = KCH * 128
            nq = 4 if GP % 4 == 0 else 2
            qq = K8 // nq
            w4n = KB // nq
            stage = har[:, 0:KB]
            qoff = KB // 2
            qbuf = har_i[:, qoff:qoff + K8]
            woff = qoff * 2 + K8 * 2
            wbfT = har[:, woff:woff + KB]
            w4off = woff + KB
            w4q = har_i[:, w4off // 2: w4off // 2 + w4n]
            assert (w4off // 2 + w4n) * 4 <= FC * TT * 2, "har overflow"
            zsc = war_f[:, 0:GP]
            ssc = war_f[:, 128:128 + GP]
            zraw = war_i[:, 512:528]
            zint = war_i[:, 640:768]
            zf = war_f[:, 1024:1152]
            sfc = war_f[:, 1280:1408]

            tp_done = 0
            q_war_dve = 0
            q_war_pe = 0
            W("sync", sDVE)
            W("sync", sACT)
            W("sync", sPE)
            for ncb in range(NCH):
                if ncb >= 1:
                    W("sync", sDVE, q_war_dve)
                    W("sync", sPE, q_war_pe)
                dma("sync", zraw[0:GP, :], qz[:, ncb * 16:(ncb + 1) * 16], dQ)
                dma("sync", sfc[0:GP, :], s_nat[:, ncb * 128:(ncb + 1) * 128], dQ)
                dma("sync", qbuf[:, :], qwT[ncb * 128:(ncb + 1) * 128, :], dQ)
                W("vector", dQ)
                W("vector", sACT)
                for j in range(8):
                    I("vector", lambda e, o=zint[0:GP, j::8], i_=zraw[0:GP, :], jj=j:
                      e.tensor_scalar(o, i_, 4 * jj, 15, ALU.logical_shift_right,
                                      ALU.bitwise_and), sDVE, ss=(j == 0))
                I("vector", lambda e: e.scalar_tensor_tensor(
                    zf[0:GP, :], zint[0:GP, :], 1.0, sfc[0:GP, :], ALU.add, ALU.mult),
                  sDVE)
                bz = B[6]
                bz.pre_write()
                W("tensor", sDVE)
                I("tensor", lambda e: e.transpose(bz.f[:, 0:GP], zf[0:GP, :],
                                                  c_idf[0:GP, 0:GP]), sPE)
                I("tensor", lambda e: e.transpose(bz.f[:, 128:128 + GP], sfc[0:GP, :],
                                                  c_idf[0:GP, 0:GP]), sPE)
                W("scalar", sPE)
                W("scalar", sDVE)
                I("scalar", lambda e: e.copy(zsc, bz.f[:, 0:GP]), sACT)
                I("scalar", lambda e: e.copy(ssc, bz.f[:, 128:128 + GP]), sACT)
                bz.rd(sACT)
                W("vector", dQ)
                W("vector", sACT)
                if ncb >= 1:
                    W("vector", sPE, tp_done)
                for q in range(nq):
                    for j in range(8):
                        I("vector", lambda e, o=w4q[:, j::8], qs=q * qq, jj=j:
                          e.tensor_scalar(o, qbuf[:, qs:qs + qq], 4 * jj, 15,
                                          ALU.logical_shift_right, ALU.bitwise_and),
                          sDVE, ss=(j == 0))
                    g0 = q * (GP // nq)
                    for gl in range(GP // nq):
                        g = g0 + gl
                        I("vector", lambda e, gg=g, gloc=gl:
                          e.tensor_scalar(wbfT[:, gg * 128:(gg + 1) * 128],
                                          w4q[:, gloc * 128:(gloc + 1) * 128],
                                          ssc[:, gg:gg + 1], zsc[:, gg:gg + 1],
                                          ALU.mult, ALU.subtract), sDVE,
                          ss=(gl == 0))
                q_war_dve = sDVE.val
                W("scalar", dSO)
                for g in range(GP):
                    bk = B[6 + (g % 2)]
                    bk.pre_write()
                    W("tensor", sDVE)
                    I("tensor", lambda e, gg=g, o=bk.b[:, 0:128]:
                      e.transpose(o, wbfT[:, gg * 128:(gg + 1) * 128], c_idb[:, :]),
                      sPE)
                    W("scalar", sPE)
                    I("scalar", lambda e, gg=g, i_=bk.b[:, 0:128]:
                      e.copy(stage[:, gg * 128:(gg + 1) * 128], i_), sACT,
                      ss=(g < 2))
                    bk.rd(sACT)
                tp_done = sPE.val
                q_war_pe = sPE.val
                W("gpsimd", sACT)
                dma("gpsimd", scratch_w[ncb], stage, dSO)

        set_phase("p_dq")
        emit_dequant(i_qwg, i_zg, i_sg, sWg, FC, KC)
        emit_dequant(i_qwu, i_zu, i_su, sWu, FC, KC)
        emit_dequant(i_qwd, i_zd, i_sd, sWd, KC, FC)

        for e in ENGS:
            gate_all(e)

        state = {"xsl_war": {}, "s52_war": {}}

        def ln_stats_pass(src):
            set_phase("ln")
            B[0].pre_write()
            B[1].pre_write()
            W("sync", dXW[0])
            W("sync", dXW[1])
            W("vector", sPE)
            W("scalar", sPE)
            xw = dict(state["xsl_war"])
            mmv = 0
            for kc in range(KC):
                pb = kc % 2
                if pb in xw:
                    W("sync", sDVE, xw[pb][0])
                    if xw[pb][1]:
                        W("sync", sACT, xw[pb][1])
                dma("sync", xsl[pb][:, :], src[kc * 128:(kc + 1) * 128, :], dXS[pb])
                W("vector", dXS[pb])
                if kc >= 1:
                    W("vector", sPE, mmv)
                I("vector", lambda e, p=pb: e.tensor_copy(xbf[:, :], xsl[p][:, :]),
                  sDVE)
                W("scalar", dXS[pb])
                if kc >= 1:
                    W("scalar", sPE, mmv)
                I("scalar", lambda e, p=pb: e.activation(
                    sqbf[:, :], xsl[p][:, :], AF.Square), sACT)
                xw[pb] = (sDVE.val, sACT.val)
                W("tensor", sDVE)
                W("tensor", sACT)
                I("tensor", lambda e, st=(kc == 0), sp=(kc == KC - 1):
                  e.matmul(B[0].f[0:1, :], c_onescol[:, :], xbf[:, :],
                           start=st, stop=sp), sPE)
                I("tensor", lambda e, st=(kc == 0), sp=(kc == KC - 1):
                  e.matmul(B[1].f[0:1, :], c_onescol[:, :], sqbf[:, :],
                           start=st, stop=sp), sPE)
                mmv = sPE.val
            state["xsl_war"] = xw

        def ln_rows_and_bcast():
            set_phase("ln")
            W("scalar", sPE)
            I("scalar", lambda e: e.activation(rowA, B[0].f[0:1, :], AF.Identity,
                                               scale=1.0 / HID), sACT)
            I("scalar", lambda e: e.activation(rowB, B[0].f[0:1, :], AF.Square,
                                               scale=1.0 / HID), sACT)
            B[0].rd(sACT)
            W("vector", sACT)
            I("vector", lambda e: e.scalar_tensor_tensor(
                rowB, B[1].f[0:1, :], 1.0 / HID, rowB, ALU.mult, ALU.subtract), sDVE)
            B[1].rd(sDVE)
            W("scalar", sDVE)
            I("scalar", lambda e: e.activation(rowB, rowB, AF.Ln, bias=c_eps[0:1, 0:1]), sACT)
            I("scalar", lambda e: e.activation(rowB, rowB, AF.Exp, scale=-0.5), sACT)
            I("scalar", lambda e: e.activation(rows_abf, rowB, AF.Identity), sACT)
            W("vector", sACT)
            I("vector", lambda e: e.scalar_tensor_tensor(
                rowA, rowA, -1.0, rowB, ALU.mult, ALU.mult), sDVE)
            W("scalar", sDVE)
            I("scalar", lambda e: e.activation(rows_bbf, rowA, AF.Identity), sACT)
            B[2].pre_write()
            B[3].pre_write()
            W("tensor", sACT)
            I("tensor", lambda e: e.matmul(B[2].f[:, :], c_onesb[:, :], rows_abf,
                                           start=True, stop=True), sPE)
            I("tensor", lambda e: e.matmul(B[3].f[:, :], c_onesb[:, :], rows_bbf,
                                           start=True, stop=True), sPE)

        def ln_build_xn(src, xn_free_pe):
            set_phase("ln")
            W("vector", sPE, xn_free_pe)
            W("sync", dXW[0])
            W("sync", dXW[1])
            xw = dict(state["xsl_war"])
            for kc in range(KC):
                pb = kc % 2
                if pb in xw:
                    W("sync", sDVE, xw[pb][0])
                dma("sync", xsl[pb][:, :], src[kc * 128:(kc + 1) * 128, :], dXS[pb])
                W("vector", dXS[pb])
                W("vector", sPE)
                I("vector", lambda e, p=pb: e.tensor_tensor(
                    s52[p][:, :], xsl[p][:, :], B[2].f[:, :], ALU.mult), sDVE,
                  ss=(kc < 2))
                I("vector", lambda e, k=kc, p=pb: e.tensor_tensor(
                    xn[:, k * TT:(k + 1) * TT], s52[p][:, :], B[3].f[:, :],
                    ALU.add), sDVE)
                xw[pb] = (sDVE.val, 0)
            B[2].rd(sDVE)
            B[3].rd(sDVE)
            state["xsl_war"] = xw

        def residual_evac(bank, mc, dst_sl, scale, fuse_stats, nst):
            pb = mc % 2
            W("vector", dXS[pb])
            W("vector", sPE)
            if scale == 1.0:
                I("vector", lambda e: e.tensor_tensor(
                    s52[pb][:, :], xsl[pb][:, :], bank.f[:, :], ALU.add), sDVE)
            else:
                I("vector", lambda e: e.scalar_tensor_tensor(
                    s52[pb][:, :], bank.f[:, :], scale, xsl[pb][:, :],
                    ALU.mult, ALU.add), sDVE)
            bank.rd(sDVE)
            W("gpsimd", sDVE)
            dma("gpsimd", dst_sl[mc * 128:(mc + 1) * 128, :], s52[pb][:, :], dXW[pb])
            if fuse_stats:
                if mc == 0:
                    B[0].pre_write()
                    B[1].pre_write()
                if mc >= 1:
                    W("vector", sPE, nst[0])
                I("vector", lambda e, p=pb: e.tensor_copy(xbf[:, :], s52[p][:, :]),
                  sDVE)
                if mc >= 1:
                    W("scalar", sPE, nst[0])
                W("scalar", sDVE)
                I("scalar", lambda e, p=pb: e.activation(
                    sqbf[:, :], s52[p][:, :], AF.Square), sACT)
                W("tensor", sDVE)
                W("tensor", sACT)
                I("tensor", lambda e, st=(mc == 0), sp=(mc == KC - 1):
                  e.matmul(B[0].f[0:1, :], c_onescol[:, :], xbf[:, :],
                           start=st, stop=sp), sPE)
                I("tensor", lambda e, st=(mc == 0), sp=(mc == KC - 1):
                  e.matmul(B[1].f[0:1, :], c_onescol[:, :], sqbf[:, :],
                           start=st, stop=sp), sPE)
                nst[0] = sPE.val
            return (sDVE.val, 0)

        def emit_ffn(t0, src, dst_sl, fuse_stats):
            set_phase("ffn_gu")
            gate_all("sync")
            guw = {}
            for fc in range(FC):
                pb = fc % 2
                gB, uB = B[4 + pb], B[6 + pb]
                gsl = war[:, pb * 4096: pb * 4096 + 4096]
                usl = war[:, 8192 + pb * 4096: 8192 + pb * 4096 + 4096]
                if pb in guw:
                    W("sync", sPE, guw[pb])
                dma("sync", gsl, sWg[fc], dWG[pb])
                dma("sync", usl, sWu[fc], dWU[pb])
                gB.pre_write()
                uB.pre_write()
                W("tensor", dWG[pb])
                W("tensor", dWU[pb])
                mm_accum(gB, [gsl[:, k * 128:(k + 1) * 128] for k in range(KC)],
                         [xn[:, k * TT:(k + 1) * TT] for k in range(KC)])
                mm_accum(uB, [usl[:, k * 128:(k + 1) * 128] for k in range(KC)],
                         [xn[:, k * TT:(k + 1) * TT] for k in range(KC)])
                guw[pb] = sPE.val
                hsl = har[:, fc * TT:(fc + 1) * TT]
                W("scalar", sPE)
                I("scalar", lambda e, o=hsl, b_=gB: e.activation(o, b_.f[:, :],
                                                                 AF.Silu), sACT)
                gB.rd(sACT)
                W("vector", sACT)
                I("vector", lambda e, o=hsl, b_=uB: e.tensor_tensor(
                    o, o, b_.f[:, :], ALU.mult), sDVE)
                uB.rd(sDVE)
            h_done = sDVE.val
            set_phase("ffn_dn")
            dsw = {}
            nst = [0]
            QW = 2752
            dsw = {}
            for mc in range(KC):
                pb = mc % 2
                dB = B[4 + pb]
                bufs = []
                for half in range(2):
                    th = (2 * mc + half) % 3
                    dsl = war[:, th * 5504: th * 5504 + 5504]
                    bufs.append(dsl)
                    if th in dsw:
                        W("sync", sPE, dsw[th])
                    elif mc == 0 and half == 0:
                        W("sync", sPE, guw[0])
                        W("sync", sPE, guw[1])
                    dma("sync", dsl[:, 0:2752], sWd[mc][:, half * 5504:half * 5504 + 2752], dWD[th])
                    dma("sync", dsl[:, 2752:5504],
                        sWd[mc][:, half * 5504 + 2752:half * 5504 + 5504], dWD[th])
                xwp = state["xsl_war"].get(pb)
                if xwp is not None:
                    W("sync", sDVE, xwp[0])
                dma("sync", xsl[pb][:, :], src[mc * 128:(mc + 1) * 128, t0:t0 + TT],
                    dXS[pb])
                dB.pre_write()
                W("tensor", dWD[(2 * mc) % 3])
                W("tensor", dWD[(2 * mc + 1) % 3])
                W("tensor", sDVE, h_done)
                lhs, rhs = [], []
                for k in range(FC):
                    half = k // 43
                    kk = k % 43
                    lhs.append(bufs[half][:, kk * 128:(kk + 1) * 128])
                    rhs.append(har[:, k * TT:(k + 1) * TT])
                mm_accum(dB, lhs, rhs)
                dsw[(2 * mc) % 3] = sPE.val
                dsw[(2 * mc + 1) % 3] = sPE.val
                v = residual_evac(dB, mc, dst_sl, 0.5, fuse_stats, nst)
                state["xsl_war"][pb] = v

        def emit_inproj(t0, ssm_prev_loads):
            set_phase("inproj")
            gate_all("sync")
            W("gpsimd", dHS)
            W("gpsimd", dSG)
            ipw = {}
            gw_war = {}
            for mc in range(MCI):
                pb = mc % 2
                th = mc % 3
                pB = B[4 + pb]
                wsl = war[:, th * 4096: th * 4096 + 4096]
                if th in ipw:
                    W("sync", sPE, ipw[th])
                dma("sync", wsl, sWin[mc], dWI[th])
                pB.pre_write()
                W("tensor", dWI[th])
                mm_accum(pB, [wsl[:, k * 128:(k + 1) * 128] for k in range(KC)],
                         [xn[:, k * TT:(k + 1) * TT] for k in range(KC)])
                ipw[th] = sPE.val
                if mc < CCN:
                    stg = xbf if (mc % 2 == 0) else sqbf
                    if mc % 2 in gw_war:
                        W("scalar", dGS[mc % 2])
                    W("scalar", sPE)
                    I("scalar", lambda e, o=stg[:, :], b_=pB:
                      e.activation(o, b_.f[:, :], AF.Silu), sACT)
                    pB.rd(sACT)
                    W("gpsimd", sACT)
                    dma("gpsimd", sGATE[mc], stg[:, :], dGS[mc % 2])
                    gw_war[mc % 2] = 1
                elif mc < CCN + CVC:
                    cc = mc - CCN
                    tl_r = tails[(t0 // TT) % 2]
                    tl_w = tails[((t0 // TT) + 1) % 2]
                    W("vector", sPE)
                    I("vector", lambda e, b_=pB: e.tensor_copy(convin[:, 3:515],
                                                               b_.f[:, :]), sDVE)
                    I("vector", lambda e, c=cc, t_=tl_r: e.tensor_copy(
                        convin[:, 0:3], t_[:, c * 3:(c + 1) * 3]), sDVE)
                    I("vector", lambda e, c=cc, t_=tl_w, b_=pB: e.tensor_copy(
                        t_[:, c * 3:(c + 1) * 3], b_.f[:, 509:512]), sDVE)
                    pB.rd(sDVE)
                    I("vector", lambda e, c=cc: e.tensor_scalar(
                        cacc[:, :], convin[:, 0:512],
                        c_convw[:, c * 4:c * 4 + 1], None, ALU.mult), sDVE)
                    for j in range(1, 4):
                        I("vector", lambda e, c=cc, jj=j: e.scalar_tensor_tensor(
                            cacc[:, :], convin[:, jj:512 + jj],
                            c_convw[:, c * 4 + jj:c * 4 + jj + 1], cacc[:, :],
                            ALU.mult, ALU.add), sDVE)
                    W("scalar", sDVE)
                    if cc < CCN:
                        stg = xbf if (mc % 2 == 0) else sqbf
                        if mc % 2 in gw_war:
                            W("scalar", dGS[mc % 2])
                        I("scalar", lambda e, c=cc, o=stg[:, :]: e.activation(
                            o, cacc[:, :], AF.Silu, bias=c_convb[:, c:c + 1]),
                          sACT)
                        W("gpsimd", sACT)
                        dma("gpsimd", sHS[cc], stg[:, :], dGS[mc % 2])
                        gw_war[mc % 2] = 1
                    else:
                        gidx = cc - CCN
                        dstt = bt_t if gidx < NG else ct_t
                        gg = gidx % NG
                        I("scalar", lambda e, c=cc, d_=dstt, g_=gg: e.activation(
                            d_[:, g_ * TT:(g_ + 1) * TT], cacc[:, :], AF.Silu,
                            bias=c_convb[:, c:c + 1]), sACT)
                else:
                    dsc = s52[0][:, :]
                    W("scalar", sPE)
                    W("scalar", sDVE)
                    I("scalar", lambda e, b_=pB: e.activation(
                        dsc, b_.f[:, :], AF.Abs, bias=c_dtb[:, 0:1]), sACT)
                    I("scalar", lambda e: e.activation(dsc, dsc, AF.Exp, scale=-1.0),
                      sACT)
                    I("scalar", lambda e: e.activation(dsc, dsc, AF.Ln, bias=1.0),
                      sACT)
                    I("scalar", lambda e, b_=pB: e.activation(
                        adt[:, :], b_.f[:, :], AF.Relu, bias=c_dtb[:, 0:1]), sACT)
                    pB.rd(sACT)
                    W("vector", sACT)
                    I("vector", lambda e: e.tensor_tensor(dt_f[:, :], dsc, adt[:, :],
                                                          ALU.add), sDVE)
                    I("vector", lambda e: e.tensor_scalar(dt_f[:, :], dt_f[:, :],
                                                          100.0, None, ALU.min), sDVE)
                    I("vector", lambda e: e.tensor_scalar(
                        adt[:, :], dt_f[:, :], c_eA[:, 0:1], -1.0, ALU.mult, ALU.mult),
                      sDVE)
            return sPE.val

        def emit_ssm(t0, st_out):
            hs_ch = har[:, 32768:40960]
            xch = xn[:, 0:8192]
            sg_ch = xn[:, 8192:16384]
            ysb = war[:, 0:8192]
            ltf = [war_f[:, 4096:4608], war_f[:, 4608:5120]]
            lte = [war[:, 10240:10752], war[:, 10752:11264]]
            mtb = [war[:, 11264:11776], war[:, 11776:12288]]
            ebc = [war_f[:, 6144:6656], war_f[:, 6656:7168]]
            cew = [war[:, 14336:14848], war[:, 14848:15360]]

            set_phase("ssm")
            gate_all("vector")
            gate_all("scalar")
            gate_all("tensor")
            st_war = {}
            facr_rd = 0
            loads_hi = 0
            for z in range(CPT):
                zg = (t0 // TT) * CPT + z
                zsl = slice(z * CH, (z + 1) * CH)
                for e in ("vector", "scalar", "tensor"):
                    W(e, sDVE)
                    W(e, sACT)
                    W(e, sPE)
                I("vector", lambda e, zs=zsl: e.tensor_tensor_scan(
                    acum, adt[:, zs], adt[:, zs], 0.0, ALU.add, ALU.bypass), sDVE)
                bk = B[6]
                bk.pre_write()
                W("tensor", sDVE)
                I("tensor", lambda e: e.transpose(bk.f[:, 0:128], acum, c_idf[:, :]),
                  sPE)
                W("scalar", sPE)
                I("scalar", lambda e: e.copy(acumT, bk.f[:, 0:128]), sACT)
                bk.rd(sACT)
                I("vector", lambda e: e.tensor_scalar(
                    tmp128, acum, acum[:, 127:128], None, ALU.subtract), sDVE)
                W("scalar", sDVE)
                I("scalar", lambda e: e.activation(tmp128, tmp128, AF.Exp, scale=-1.0),
                  sACT)
                bk2 = B[7]
                bk2.pre_write()
                W("tensor", sACT)
                I("tensor", lambda e: e.transpose(bk2.f[:, 0:128], tmp128,
                                                  c_idf[:, :]), sPE)
                W("scalar", sPE)
                I("scalar", lambda e: e.copy(decT, bk2.f[:, 0:128]), sACT)
                bk2.rd(sACT)
                bk.pre_write()
                I("tensor", lambda e, zs=zsl: e.transpose(bk.f[:, 0:128],
                                                          dt_f[:, zs], c_idf[:, :]),
                  sPE)
                W("scalar", sPE)
                I("scalar", lambda e: e.copy(dtT, bk.f[:, 0:128]), sACT)
                bk.rd(sACT)
                W("vector", sACT)
                I("vector", lambda e: e.tensor_tensor(coldec[:, :], dtT, decT,
                                                      ALU.mult), sDVE)
                bk2.pre_write()
                I("tensor", lambda e: e.transpose(bk2.f[0:1, 0:128], acum[:, 127:128],
                                                  c_idf[:, :]), sPE)
                W("scalar", sPE)
                I("scalar", lambda e: e.copy(rowC[0:1, 0:128], bk2.f[0:1, 0:128]),
                  sACT)
                bk2.rd(sACT)
                bk2.pre_write()
                W("tensor", sACT)
                I("tensor", lambda e: e.matmul(bk2.f[:, 0:128], c_onesf[:, :],
                                               rowC[0:1, 0:128], start=True,
                                               stop=True), sPE)
                W("scalar", sPE)
                I("scalar", lambda e: e.activation(eab[:, :], bk2.f[:, 0:128],
                                                   AF.Exp), sACT)
                bk2.rd(sACT)
                W("gpsimd", sDVE)
                W("gpsimd", dFA)
                dma("gpsimd", sAC[:, :], acum, dAC)
                W("sync", dGS[0])
                W("sync", dGS[1])
                W("sync", sDVE)
                W("sync", sACT)
                W("sync", sPE)
                dma("sync", hs_ch.rearrange("p (cc l) -> p cc l", l=CH),
                    sHS[:, :, zsl].rearrange("cc p l -> p cc l"), dHS)
                dma("sync", sg_ch.rearrange("p (cc l) -> p cc l", l=CH),
                    sGATE[:, :, zsl].rearrange("cc p l -> p cc l"), dSG)
                W("tensor", dHS)
                for cc in range(CCN):
                    bb = B[6 + cc % 2]
                    bb.pre_write()
                    I("tensor", lambda e, c=cc, o=bb.b[:, 0:128]:
                      e.transpose(o, hs_ch[:, c * 128:(c + 1) * 128], c_idb[:, :]),
                      sPE)
                    if cc % 2 == 0:
                        W("scalar", sPE)
                        I("scalar", lambda e, c=cc, i_=bb.b[:, 0:128]:
                          e.copy(xch[:, c * 128:(c + 1) * 128], i_), sACT,
                          ss=(cc < 2))
                        bb.rd(sACT)
                    else:
                        W("vector", sPE)
                        I("vector", lambda e, c=cc, i_=bb.b[:, 0:128]:
                          e.tensor_copy(xch[:, c * 128:(c + 1) * 128], i_), sDVE,
                          ss=(cc < 2))
                        bb.rd(sDVE)
                xch_v, xch_a = sDVE.val, sACT.val
                qt_war = {}
                for g in range(NG):
                    ssl = st_sl[g % 2]
                    if g % 2 in st_war:
                        W("sync", dSSw[g % 2])
                        W("sync", sDVE, st_war[g % 2])
                    if zg == 0:
                        W("gpsimd", sDVE, st_war.get(g % 2, 0))
                        W("gpsimd", dSSw[g % 2])
                        I("gpsimd", lambda e, s=ssl: e.memset(s[:, :], 0.0), sGC)
                        st_ld = (sGC, sGC.val)
                    else:
                        dma("sync", ssl[:, :], sST[:, g * 1024:(g + 1) * 1024],
                            dSSr[g % 2])
                        st_ld = (dSSr[g % 2], dSSr[g % 2].val)
                    W("gpsimd", dAC)
                    W("gpsimd", sPE, facr_rd)
                    facsrc = sAC.rearrange("p l -> (p l)").unsqueeze(0)
                    dma("gpsimd", facr, facsrc[:, g * 2048:(g + 1) * 2048], dFA)
                    bb = B[6 + g % 2]
                    bb.pre_write()
                    I("tensor", lambda e, o=bb.f[:, 0:128],
                      l_=bt_t[:, g * TT + z * CH: g * TT + z * CH + CH],
                      r_=ct_t[:, g * TT + z * CH: g * TT + z * CH + CH]:
                      e.matmul(o, l_, r_, start=True, stop=True), sPE)
                    W("vector", sPE)
                    I("vector", lambda e, b_=bb: e.tensor_tensor(
                        cbt[:, :], b_.f[:, 0:128], c_ut[:, :], ALU.mult), sDVE)
                    bb.rd(sDVE)
                    for q in range(4):
                        pq = q % 2
                        qb = B[2 + pq]
                        qb.pre_write()
                        W("tensor", dFA)
                        I("tensor", lambda e, qq=q, o=qb.f[:, :]:
                          e.matmul(o, c_onesf[:, :], facr[:, qq * 512:(qq + 1) * 512],
                                   start=True, stop=True), sPE)
                        h0 = g * 16 + q * 4
                        W("vector", sPE)
                        I("vector", lambda e, o=ltf[pq], b_=qb, hh=h0:
                          e.tensor_tensor(
                              o.rearrange("p (a b) -> p a b", a=4),
                              b_.f[:, :].rearrange("p (a b) -> p a b", a=4),
                              acumT[:, hh:hh + 4].unsqueeze(2).broadcast_to(
                                  [128, 4, 128]), ALU.subtract), sDVE)
                        I("vector", lambda e, o=ltf[pq]: e.tensor_scalar(
                            o, o, 0.0, None, ALU.min), sDVE)
                        W("scalar", sPE)
                        W("scalar", sDVE)
                        I("scalar", lambda e, o=ebc[pq], b_=qb:
                          e.activation(o, b_.f[:, :], AF.Exp), sACT)
                        qb.rd(sACT)
                        qb.rd(sDVE)
                        W("scalar", sDVE)
                        I("scalar", lambda e, o=lte[pq], i_=ltf[pq]:
                          e.activation(o, i_, AF.Exp), sACT)
                        W("vector", sACT)
                        for hh in range(4):
                            I("vector", lambda e, o=mtb[pq], i_=lte[pq], h=h0 + hh,
                              hl=hh: e.scalar_tensor_tensor(
                                  o[:, hl * 128:(hl + 1) * 128],
                                  i_[:, hl * 128:(hl + 1) * 128],
                                  dtT[:, h:h + 1], cbt[:, :], ALU.mult, ALU.mult),
                              sDVE, ss=(hh == 0))
                        I("vector", lambda e, o=cew[pq], eb_=ebc[pq],
                          ct_=ct_t[:, g * TT + z * CH: g * TT + z * CH + CH]
                          .unsqueeze(1).broadcast_to([128, 4, 128]):
                          e.tensor_tensor(
                              o.rearrange("p (a b) -> p a b", a=4), ct_,
                              eb_.rearrange("p (a b) -> p a b", a=4), ALU.mult), sDVE)
                        if zg > 0:
                            W("vector", st_ld[0], st_ld[1])
                            if pq in qt_war:
                                W("vector", sPE, qt_war[pq])
                            I("vector", lambda e, o=stbf[pq], s=ssl, qq=q:
                              e.tensor_copy(o[:, :], s[:, qq * 256:(qq + 1) * 256]),
                              sDVE)
                        yB = B[4 + (q // 2)]
                        if q == 0 or q == 2:
                            yB.pre_write()
                        W("tensor", sDVE)
                        for hh in range(4):
                            hglob = h0 + hh
                            off = ((hglob % 8)) * 64
                            I("tensor", lambda e, o=yB.f[:, :], m=mtb[pq], hl=hh,
                              hg=hglob, of=off, sp=(zg == 0):
                              e.matmul(o[:, of:of + 64],
                                       m[:, hl * 128:(hl + 1) * 128],
                                       xch[:, hg * 64:(hg + 1) * 64], start=True,
                                       stop=sp), sPE if zg == 0 else None)
                            if zg > 0:
                                I("tensor", lambda e, o=yB.f[:, :], c=cew[pq], hl=hh,
                                  sb_=stbf[pq], of=off:
                                  e.matmul(o[:, of:of + 64],
                                           c[:, hl * 128:(hl + 1) * 128],
                                           sb_[:, hl * 64:(hl + 1) * 64],
                                           start=False, stop=True), sPE)
                        qt_war[pq] = sPE.val
                        if q % 2 == 1:
                            W("scalar", sPE)
                            I("scalar", lambda e, gg=g, qh=q // 2, b_=yB:
                              e.copy(ysb[:, gg * 1024 + qh * 512:
                                         gg * 1024 + qh * 512 + 512], b_.f[:, :]),
                              sACT)
                            yB.rd(sACT)
                    facr_rd = sPE.val
                    W("vector", sDVE, xch_v)
                    W("vector", sACT, xch_a)
                    W("vector", sPE)
                    for hh in range(16):
                        h = g * 16 + hh
                        I("vector", lambda e, h_=h: e.tensor_scalar(
                            xch[:, h_ * 64:(h_ + 1) * 64],
                            xch[:, h_ * 64:(h_ + 1) * 64],
                            coldec[:, h_:h_ + 1], None, ALU.mult), sDVE,
                          ss=(hh == 0))
                    bb = B[6 + g % 2]
                    bb.pre_write()
                    W("tensor", sDVE)
                    I("tensor", lambda e, o=bb.b[:, 0:128],
                      l_=bt_t[:, g * TT + z * CH: g * TT + z * CH + CH]:
                      e.transpose(o, l_, c_idb[:, :]), sPE)
                    W("vector", sPE)
                    I("vector", lambda e, b_=bb: e.tensor_copy(b_lg[:, :],
                                                               b_.b[:, 0:128]), sDVE)
                    bb.rd(sDVE)
                    for half in range(2):
                        sb_ = B[6 + (g + 1) % 2]
                        sb_.pre_write()
                        W("tensor", sDVE)
                        I("tensor", lambda e, o=sb_.f[:, :], gg=g, hf=half:
                          e.matmul(o, b_lg[:, :],
                                   xch[:, (gg * 16 + hf * 8) * 64:
                                       (gg * 16 + hf * 8 + 8) * 64],
                                   start=True, stop=True), sPE)
                        W("vector", st_ld[0], st_ld[1])
                        W("vector", sPE)
                        for hh in range(8):
                            h = g * 16 + half * 8 + hh
                            I("vector", lambda e, h_=h, b2=sb_, hl=hh, s=ssl,
                              hf=half: e.scalar_tensor_tensor(
                                  s[:, (hf * 8 + hl) * 64:(hf * 8 + hl + 1) * 64],
                                  s[:, (hf * 8 + hl) * 64:(hf * 8 + hl + 1) * 64],
                                  eab[:, h_:h_ + 1],
                                  b2.f[:, hl * 64:(hl + 1) * 64],
                                  ALU.mult, ALU.add), sDVE, ss=(hh == 0))
                        sb_.rd(sDVE)
                    W("gpsimd", sDVE)
                    dma("gpsimd", sST[:, g * 1024:(g + 1) * 1024], ssl[:, :],
                        dSSw[g % 2])
                    st_war[g % 2] = sDVE.val
                    W("vector", dHS)
                    W("vector", dSG)
                    for hh2 in range(8):
                        cc = g * 8 + hh2
                        bb = B[6 + hh2 % 2]
                        bb.pre_write()
                        W("tensor", sACT)
                        I("tensor", lambda e, c=cc, o=bb.b[:, 0:128]:
                          e.transpose(o, ysb[:, c * 128:(c + 1) * 128], c_idb[:, :]),
                          sPE)
                        W("vector", sPE)
                        gt = g2h[hh2 % 2]
                        I("vector", lambda e, c=cc, o=gt[:, :], b_=bb:
                          e.scalar_tensor_tensor(o, hs_ch[:, c * 128:(c + 1) * 128],
                                                 c_D[:, c:c + 1], b_.b[:, 0:128],
                                                 ALU.mult, ALU.add), sDVE)
                        bb.rd(sDVE)
                        I("vector", lambda e, c=cc, g_=gt:
                          e.tensor_tensor(sg_ch[:, c * 128:(c + 1) * 128],
                                          sg_ch[:, c * 128:(c + 1) * 128],
                                          g_[:, :], ALU.mult), sDVE)
                    W("scalar", sDVE)
                    I("scalar", lambda e, gg=g: e.activation(
                        ysb[:, gg * 1024:(gg + 1) * 1024],
                        sg_ch[:, gg * 1024:(gg + 1) * 1024], AF.Square), sACT)
                    qb = B[2 + g % 2]
                    qb.pre_write()
                    W("tensor", sACT)
                    for c8 in range(8):
                        I("tensor", lambda e, q_=qb, gg=g, c_=c8:
                          e.matmul(q_.f[0:1, 0:128], c_onescol[:, :],
                                   ysb[:, gg * 1024 + c_ * 128:
                                       gg * 1024 + c_ * 128 + 128],
                                   start=(c_ == 0), stop=(c_ == 7)),
                          sPE if c8 == 7 else None)
                    W("scalar", sPE)
                    I("scalar", lambda e, b_=qb: e.activation(
                        rowC[0:1, 0:128], b_.f[0:1, 0:128], AF.Ln,
                        bias=c_eps[0:1, 0:1], scale=1.0 / 1024.0), sACT)
                    qb.rd(sACT)
                    I("scalar", lambda e: e.activation(rstd_bf, rowC[0:1, 0:128],
                                                       AF.Exp, scale=-0.5), sACT)
                    rb = B[2 + (g + 1) % 2]
                    rb.pre_write()
                    W("tensor", sACT)
                    I("tensor", lambda e, o=rb.f[:, 0:128]:
                      e.matmul(o, c_onesb[:, :], rstd_bf, start=True, stop=True), sPE)
                    W("vector", sPE)
                    for c8 in range(8):
                        cc = g * 8 + c8
                        I("vector", lambda e, c=cc, b_=rb,
                          o=har[:, cc * TT + z * CH: cc * TT + z * CH + CH]:
                          e.tensor_tensor(
                              o, sg_ch[:, c * 128:(c + 1) * 128], b_.f[:, 0:128],
                              ALU.mult), sDVE, ss=(c8 == 0))
                    rb.rd(sDVE)
            return 0

        def emit_outproj(t0, src, dst_sl):
            set_phase("outproj")
            gate_all("sync")
            opw = {}
            nst = [0]
            for mc in range(KC):
                pb = mc % 2
                oB = B[4 + pb]
                wsl = war[:, pb * 8192: pb * 8192 + 8192]
                if pb in opw:
                    W("sync", sPE, opw[pb])
                dma("sync", wsl[:, 0:4096], sWout[mc][:, 0:4096], dWO[pb])
                dma("sync", wsl[:, 4096:8192], sWout[mc][:, 4096:8192], dWO[pb])
                xwp = state["xsl_war"].get(pb)
                if xwp is not None:
                    W("sync", sDVE, xwp[0])
                dma("sync", xsl[pb][:, :], src[mc * 128:(mc + 1) * 128, t0:t0 + TT],
                    dXS[pb])
                oB.pre_write()
                W("tensor", dWO[pb])
                mm_accum(oB, [wsl[:, k * 128:(k + 1) * 128] for k in range(CCN)],
                         [har[:, k * TT:(k + 1) * TT] for k in range(CCN)])
                opw[pb] = sPE.val
                v = residual_evac(oB, mc, dst_sl, 1.0, True, nst)
                state["xsl_war"][pb] = v

        st_out = {}
        ssm_loads = 0
        for t in range(ntiles):
            t0 = t * TT
            if upto == "prologue":
                break
            ln_stats_pass(i_xT[:, t0:t0 + TT])
            ln_rows_and_bcast()
            ln_build_xn(i_xT[:, t0:t0 + TT], sPE.val)
            emit_ffn(t0, i_xT, sX[:, t0:t0 + TT], fuse_stats=True)
            if upto == "ffn1":
                break
            ln_rows_and_bcast()
            ln_build_xn(sX[:, t0:t0 + TT], sPE.val)
            ip_done = emit_inproj(t0, ssm_loads)
            if upto == "inproj":
                break
            ssm_loads = emit_ssm(t0, st_out)
            if upto == "ssm":
                break
            emit_outproj(t0, sX, sX[:, t0:t0 + TT])
            if upto == "mamba":
                break
            ln_rows_and_bcast()
            ln_build_xn(sX[:, t0:t0 + TT], sPE.val)
            emit_ffn(t0, sX, o_xT[:, t0:t0 + TT], fuse_stats=False)

        for s_ in DMASEMS:
            W("sync", s_)

        with nc.Block() as block:
            def mk(name):
                def body(eng):
                    cur = None
                    sid = None
                    for ph, fn in sched[name]:
                        if ph != cur:
                            if cur is not None:
                                nc.leave_named_scope(cur, sid, False)
                            sid, _ = nc.enter_named_scope(ph, False)
                            cur = ph
                        fn(eng)
                    if cur is not None:
                        nc.leave_named_scope(cur, sid, False)
                return body
            block.sync(mk("sync"))
            block.gpsimd(mk("gpsimd"))
            block.vector(mk("vector"))
            block.scalar(mk("scalar"))
            block.tensor(mk("tensor"))

    return nc


_BUILT = {}


def _prep(inputs):
    x = np.asarray(inputs["L0_attn_x"])
    xT = np.ascontiguousarray(np.transpose(x, (0, 2, 1)))
    conv_w = np.asarray(inputs["L0_attn_conv_w"]).reshape(CONVD, 4)
    convw = np.ascontiguousarray(
        conv_w.reshape(CVC, 128, 4).transpose(1, 0, 2).reshape(128, CVC * 4))
    convb = np.ascontiguousarray(
        np.asarray(inputs["L0_attn_conv_b"]).reshape(CVC, 128).T)
    nw = np.ascontiguousarray(
        np.asarray(inputs["L0_attn_norm_w"]).reshape(CCN, 128).T)
    Dr = np.ascontiguousarray(
        np.repeat(np.asarray(inputs["L0_attn_D"]), PD).reshape(CCN, 128).T)
    shared = {
        "i_qwg": np.ascontiguousarray(np.asarray(inputs["L0_ffn_gate_qweight"]).T),
        "i_qwu": np.ascontiguousarray(np.asarray(inputs["L0_ffn_up_qweight"]).T),
        "i_qwd": np.ascontiguousarray(np.asarray(inputs["L0_ffn_down_qweight"]).T),
        "i_sg": np.ascontiguousarray(inputs["L0_ffn_gate_scales"]),
        "i_su": np.ascontiguousarray(inputs["L0_ffn_up_scales"]),
        "i_sd": np.ascontiguousarray(inputs["L0_ffn_down_scales"]),
        "i_zg": np.ascontiguousarray(inputs["L0_ffn_gate_qzeros"]),
        "i_zu": np.ascontiguousarray(inputs["L0_ffn_up_qzeros"]),
        "i_zd": np.ascontiguousarray(inputs["L0_ffn_down_qzeros"]),
        "i_winT": np.ascontiguousarray(np.asarray(inputs["L0_attn_in_proj_w"]).T),
        "i_woutT": np.ascontiguousarray(np.asarray(inputs["L0_attn_out_proj_w"]).T),
        "i_convw": convw.astype(np.float32),
        "i_convb": convb.astype(np.float32),
        "i_nw": nw.astype(np.float32),
        "i_D": Dr.astype(np.float32),
        "i_dtb": np.asarray(inputs["L0_attn_dt_bias"]).reshape(128, 1).astype(np.float32),
        "i_alog": np.asarray(inputs["L0_attn_A_log"]).reshape(128, 1).astype(np.float32),
        "i_idb": np.eye(128, dtype=ml_dtypes.bfloat16),
        "i_idf": np.eye(128, dtype=np.float32),
        "i_onesb": np.ones((1, 128), dtype=ml_dtypes.bfloat16),
        "i_onesf": np.ones((1, 128), dtype=np.float32),
        "i_onescol": np.ones((128, 1), dtype=ml_dtypes.bfloat16),
        "i_ut": np.triu(np.ones((128, 128), np.float32)).astype(ml_dtypes.bfloat16),
        "i_eps": np.full((128, 1), EPS, np.float32),
    }
    in_maps = []
    for c in range(NCORES):
        m = dict(shared)
        m["i_xT"] = np.ascontiguousarray(xT[c])
        in_maps.append(m)
    return in_maps


def kernel(**inputs):
    in_maps = _prep(inputs)
    if "nc" not in _BUILT:
        _BUILT["nc"] = build_program()
    nc = _BUILT["nc"]
    res = run_bass_kernel_spmd(nc, in_maps, core_ids=list(range(NCORES)))
    out = np.stack([np.asarray(r["o_xT"], dtype=np.float32).T for r in res.results])
    return np.ascontiguousarray(out)


# version 05
# speedup vs baseline: 1.1707x; 1.0852x over previous
import numpy as np
import ml_dtypes
import concourse.bass as bass
import concourse.mybir as mybir
from concourse.bass_utils import run_bass_kernel_spmd
from contextlib import ExitStack

F32 = mybir.dt.float32
BF16 = mybir.dt.bfloat16
I32 = mybir.dt.int32
AF = mybir.ActivationFunctionType
ALU = mybir.AluOpType

NCORES = 8
SEQ = 2048
HID = 4096
KC = HID // 128
TT = 512
NTILES = SEQ // TT
CH = 128
CPT = TT // CH
FFI = 11008
FC = FFI // 128
MIN = 18560
MCI = MIN // 128
IM = 8192
CCN = IM // 128
CONVD = 10240
CVC = CONVD // 128
NG = 8
PD = 64
EPS = 1e-5

ENGS = ("sync", "gpsimd", "vector", "scalar", "tensor")


class Sem:
    __slots__ = ("h", "val", "name")

    def __init__(self, h, name):
        self.h = h
        self.val = 0
        self.name = name


def build_program(dbg=False, ntiles=NTILES, upto=None):
    nc = bass.Bass("TRN2", target_bir_lowering=False,
                   detect_race_conditions=False)

    def din_t(name, shape, dt):
        return nc.dram_tensor(name, shape, dt, kind="ExternalInput").ap()

    i_xT = din_t("i_xT", [HID, SEQ], F32)
    i_qwg = din_t("i_qwg", [FFI, 512], I32)
    i_qwu = din_t("i_qwu", [FFI, 512], I32)
    i_qwd = din_t("i_qwd", [HID, 1376], I32)
    i_sg = din_t("i_sg", [32, FFI], F32)
    i_su = din_t("i_su", [32, FFI], F32)
    i_sd = din_t("i_sd", [86, HID], F32)
    i_zg = din_t("i_zg", [32, 1376], I32)
    i_zu = din_t("i_zu", [32, 1376], I32)
    i_zd = din_t("i_zd", [86, 512], I32)
    i_winT = din_t("i_winT", [HID, MIN], F32)
    i_woutT = din_t("i_woutT", [IM, HID], F32)
    i_convw = din_t("i_convw", [128, CVC * 4], F32)
    i_convb = din_t("i_convb", [128, CVC], F32)
    i_nw = din_t("i_nw", [128, CCN], F32)
    i_D = din_t("i_D", [128, CCN], F32)
    i_dtb = din_t("i_dtb", [128, 1], F32)
    i_alog = din_t("i_alog", [128, 1], F32)
    i_idb = din_t("i_idb", [128, 128], BF16)
    i_idf = din_t("i_idf", [128, 128], F32)
    i_onesb = din_t("i_onesb", [1, 128], BF16)
    i_onesf = din_t("i_onesf", [1, 128], F32)
    i_onescol = din_t("i_onescol", [128, 1], BF16)
    i_ut = din_t("i_ut", [128, 128], BF16)
    i_eps = din_t("i_eps", [128, 1], F32)

    o_xT = nc.dram_tensor("o_xT", [HID, SEQ], F32, kind="ExternalOutput").ap()

    def scr(name, shape, dt, dbgout=False):
        if dbgout and dbg:
            return nc.dram_tensor(name, shape, dt, kind="ExternalOutput").ap()
        return nc.dram_tensor(name, shape, dt).ap()

    sX = scr("sX", [HID, SEQ], F32, dbgout=True)
    sWg = scr("sWg", [FC, 128, KC * 128], BF16, dbgout=True)
    sWu = scr("sWu", [FC, 128, KC * 128], BF16)
    sWd = scr("sWd", [KC, 128, FC * 128], BF16)
    sWin = scr("sWin", [MCI, 128, KC * 128], BF16, dbgout=True)
    sWout = scr("sWout", [KC, 128, CCN * 128], BF16)
    sGATE = scr("sGATE", [CCN, 128, TT], BF16, dbgout=True)
    sHS = scr("sHS", [CCN, 128, TT], BF16, dbgout=True)
    sAC = scr("sAC", [128, 128], F32)
    sST = scr("sST", [128, IM], F32, dbgout=True)

    with ExitStack() as ctx:
        def sb(name, shape, dt):
            return ctx.enter_context(nc.sbuf_tensor(name, shape, dt))

        c_idb = sb("c_idb", [128, 128], BF16)
        c_idf = sb("c_idf", [128, 128], F32)
        c_onesb = sb("c_onesb", [1, 128], BF16)
        c_onesf = sb("c_onesf", [1, 128], F32)
        c_onescol = sb("c_onescol", [128, 1], BF16)
        c_ut = sb("c_ut", [128, 128], BF16)
        c_convw = sb("c_convw", [128, CVC * 4], F32)
        c_convb = sb("c_convb", [128, CVC], F32)
        c_nw = sb("c_nw", [128, CCN], F32)
        c_D = sb("c_D", [128, CCN], F32)
        c_dtb = sb("c_dtb", [128, 1], F32)
        c_eA = sb("c_eA", [128, 1], F32)
        c_eps = sb("c_eps", [128, 1], F32)

        xn = sb("xn", [128, KC * TT], BF16)
        har = sb("har", [128, FC * TT], BF16)
        war = sb("war", [128, 16512], BF16)
        bt_t = sb("bt_t", [128, NG * TT], BF16)
        ct_t = sb("ct_t", [128, NG * TT], BF16)
        rows = sb("rows", [128, 2240], F32)
        tails_t = sb("tails", [128, 480], F32)
        xsl = [sb(f"xsl{i}", [128, TT], F32) for i in range(2)]
        s52 = [sb(f"s52_{i}", [128, 512], F32) for i in range(2)]
        xbf = sb("xbf", [128, TT], BF16)
        sqbf = sb("sqbf", [128, TT], BF16)
        dt_f = sb("dt_f", [128, TT], F32)
        adt = sb("adt", [128, TT], F32)
        sst = sb("sst", [128, 512], F32)
        cbt = sb("cbt", [128, 128], BF16)
        b_lg = sb("b_lg", [128, 128], BF16)
        stbf = [sb(f"stbf{i}", [128, 256], BF16) for i in range(2)]
        st_sl = [sb(f"st_sl{i}", [128, 1024], F32) for i in range(2)]
        g2h = [sb(f"g2h{i}", [128, 128], BF16) for i in range(2)]

        acum = sst[:, 0:128]
        acumT = sst[:, 128:256]
        decT = sst[:, 256:384]
        dtT = sst[:, 384:512]

        rows_bf = rows.ap().bitcast(BF16)
        rowA = rows[0:1, 0:512]
        rowB = rows[0:1, 512:1024]
        rows_abf = rows_bf[0:1, 2048:2560]
        rows_bbf = rows_bf[0:1, 2560:3072]
        facr = rows[0:1, 0:2048]
        rowC = rows[0:1, 2048:2176]
        rstd_bf = rows_bf[0:1, 4352:4480]
        tails = [tails_t[:, 240 * i: 240 * (i + 1)] for i in range(2)]

        war_f = war.ap().bitcast(F32)
        war_i = war.ap().bitcast(I32)
        har_f = har.ap().bitcast(F32)
        har_i = har.ap().bitcast(I32)
        tmp128 = har_f[:, 20480:20608]
        convin = har_f[:, 20608:21128]
        cacc = har_f[:, 21128:21640]
        eab = har_f[:, 21640:21768]
        coldec = har_f[:, 21768:21896]

        def mksem(name):
            return Sem(ctx.enter_context(nc.semaphore(name)), name)

        sGC, sPE, sACT, sDVE = (mksem(n) for n in ("sGC", "sPE", "sACT", "sDVE"))
        DMASEMS = []

        def mkdsem(name):
            s = mksem(name)
            DMASEMS.append(s)
            return s

        dK = mkdsem("dK")
        dWIN = mkdsem("dWIN")
        dF = [mkdsem(f"dF{i}") for i in range(2)]
        dO = [mkdsem(f"dO{i}") for i in range(2)]
        dQ = mkdsem("dQ")
        dSO = mkdsem("dSO")
        dXS = [mkdsem(f"dXS{i}") for i in range(2)]
        dXW = [mkdsem(f"dXW{i}") for i in range(2)]
        dWG = [mkdsem(f"dWG{i}") for i in range(2)]
        dWU = [mkdsem(f"dWU{i}") for i in range(2)]
        dWD = [mkdsem(f"dWD{i}") for i in range(3)]
        dWI = [mkdsem(f"dWI{i}") for i in range(3)]
        dWO = [mkdsem(f"dWO{i}") for i in range(2)]
        dGS = [mkdsem(f"dGS{i}") for i in range(2)]
        dHS = mkdsem("dHS")
        dSG = mkdsem("dSG")
        dSSr = [mkdsem(f"dSSr{i}") for i in range(2)]
        dSSw = [mkdsem(f"dSSw{i}") for i in range(2)]
        dFA = mkdsem("dFA")
        dAC = mkdsem("dAC")
        ALLSEMS = [sGC, sPE, sACT, sDVE] + DMASEMS

        sched = {e: [] for e in ENGS}
        wait_hi = {}
        PH = ["init"]

        def set_phase(name):
            PH[0] = name

        def W(eng, sem, val=None):
            v = sem.val if val is None else val
            if v <= 0:
                return
            assert v <= sem.val, f"wait>{sem.name} sched"
            key = (eng, sem.name)
            if wait_hi.get(key, -1) >= v:
                return
            wait_hi[key] = v
            sched[eng].append((PH[0], lambda e, s=sem.h, vv=v: e.wait_ge(s, vv)))

        SELFSEM = {"vector": sDVE, "scalar": sACT, "gpsimd": sGC}

        def I(eng, fn, sem=None, n=1, ss=True):
            own = SELFSEM.get(eng)
            if ss and own is not None and own.val > 0 and (
                    sem is not None or eng != "gpsimd"):
                W(eng, own)
            if sem is None:
                sched[eng].append((PH[0], fn))
            else:
                sem.val += n
                sched[eng].append(
                    (PH[0], lambda e, f=fn, s=sem.h, nn=n: f(e).then_inc(s, nn)))

        def dma(q, dst, src, sem):
            I(q, lambda e, d=dst, s=src: e.dma_start(d, s), sem, 16)

        def gate_all(eng):
            for s in ALLSEMS:
                W(eng, s)

        class Bank:
            def __init__(self, i):
                self.t = ctx.enter_context(nc.psum_tensor(f"ps{i}", [128, 1024], BF16))
                self.b = self.t[:, :]
                self.f = self.t.ap().bitcast(F32)
                self.readers = []

            def pre_write(self):
                for (s, v) in self.readers:
                    W("tensor", s, v)
                self.readers.clear()

            def rd(self, sem):
                self.readers.append((sem, sem.val))

        B = [Bank(i) for i in range(8)]

        def mm_accum(bank, lhsT_slices, rhs_slices, kwaits=None):
            n = len(lhsT_slices)
            for k in range(n):
                last = (k == n - 1)
                if kwaits is not None:
                    W("tensor", sDVE, kwaits[k])
                I("tensor", lambda e, o=bank.f[:, :], l=lhsT_slices[k],
                  r=rhs_slices[k], st=(k == 0), sp=last:
                  e.matmul(o, l, r, start=st, stop=sp), sPE if last else None)

        for dst, src in [(c_idb, i_idb), (c_idf, i_idf), (c_onesb, i_onesb),
                         (c_onesf, i_onesf), (c_onescol, i_onescol), (c_ut, i_ut),
                         (c_convw, i_convw), (c_convb, i_convb), (c_nw, i_nw),
                         (c_D, i_D), (c_dtb, i_dtb), (c_eps, i_eps),
                         (c_eA, i_alog)]:
            dma("sync", dst[:, :], src, dK)
        W("scalar", dK)
        W("vector", dK)
        W("tensor", dK)
        W("gpsimd", dK)
        I("scalar", lambda e: e.activation(c_eA[:, :], c_eA[:, :], AF.Exp), sACT)
        I("gpsimd", lambda e: e.memset(tails[0][:, :], 0.0), sGC)
        I("gpsimd", lambda e: e.memset(tails[1][:, :], 0.0), sGC)

        set_phase("p_conv")
        for mc in range(MCI):
            src = i_winT[:, mc * 128:(mc + 1) * 128].rearrange(
                "(kc k) m -> k kc m", k=128)
            dst = sWin[mc].rearrange("k (kc m) -> k kc m", m=128)
            dma("gpsimd", dst, src, dWIN)

        woc_dve = {}
        woc_dmao = {}
        for cc in range(CCN):
            pb = cc % 2
            fbuf = har_f[:, pb * 4096: pb * 4096 + 4096]
            bbuf = har[:, 16384 + pb * 4096: 16384 + pb * 4096 + 4096]
            if cc >= 2:
                W("sync", sDVE, woc_dve[pb])
            dma("sync", fbuf, i_woutT[cc * 128:(cc + 1) * 128, :], dF[pb])
            W("vector", dF[pb])
            if cc >= 2:
                W("vector", dO[pb])
            I("vector", lambda e, o=bbuf, i_=fbuf, c=cc:
              e.tensor_scalar(o, i_, c_nw[:, c:c + 1], None, ALU.mult), sDVE)
            woc_dve[pb] = sDVE.val
            W("gpsimd", sDVE)
            dst2 = sWout.rearrange("mc c (kc m) -> c mc kc m", m=128)[:, :, cc, :]
            dma("gpsimd", dst2, bbuf.rearrange("c (mc m) -> c mc m", m=128), dO[pb])
        W("scalar", dO[0])
        W("scalar", dO[1])
        W("vector", dO[0])
        W("vector", dO[1])

        def emit_dequant(qwT, qz, s_nat, scratch_w, NCH, KCH):
            K8 = KCH * 16
            GP = KCH
            KB = KCH * 128
            nq = 1 if GP % 4 == 0 else 2
            qq = K8 // nq
            w4n = KB // nq
            stage = har[:, 0:KB]
            qoff = KB // 2
            qbuf = har_i[:, qoff:qoff + K8]
            woff = qoff * 2 + K8 * 2
            wbfT = har[:, woff:woff + KB]
            w4off = woff + KB
            w4q = har_i[:, w4off // 2: w4off // 2 + w4n]
            assert (w4off // 2 + w4n) * 4 <= FC * TT * 2, "har overflow"
            zsc = war_f[:, 0:GP]
            ssc = war_f[:, 128:128 + GP]
            zraw = war_i[:, 512:528]
            zint = war_i[:, 640:768]
            zf = war_f[:, 1024:1152]
            sfc = war_f[:, 1280:1408]

            tp_done = 0
            q_war_dve = 0
            q_war_pe = 0
            W("sync", sDVE)
            W("sync", sACT)
            W("sync", sPE)
            for ncb in range(NCH):
                if ncb >= 1:
                    W("sync", sDVE, q_war_dve)
                    W("sync", sPE, q_war_pe)
                dma("sync", zraw[0:GP, :], qz[:, ncb * 16:(ncb + 1) * 16], dQ)
                dma("sync", sfc[0:GP, :], s_nat[:, ncb * 128:(ncb + 1) * 128], dQ)
                dma("sync", qbuf[:, :], qwT[ncb * 128:(ncb + 1) * 128, :], dQ)
                W("vector", dQ)
                W("vector", sACT)
                for j in range(8):
                    I("vector", lambda e, o=zint[0:GP, j::8], i_=zraw[0:GP, :], jj=j:
                      e.tensor_scalar(o, i_, 4 * jj, 15, ALU.logical_shift_right,
                                      ALU.bitwise_and), sDVE, ss=(j == 0))
                I("vector", lambda e: e.scalar_tensor_tensor(
                    zf[0:GP, :], zint[0:GP, :], 1.0, sfc[0:GP, :], ALU.add, ALU.mult),
                  sDVE)
                bz = B[6]
                bz.pre_write()
                W("tensor", sDVE)
                I("tensor", lambda e: e.transpose(bz.f[:, 0:GP], zf[0:GP, :],
                                                  c_idf[0:GP, 0:GP]), sPE)
                I("tensor", lambda e: e.transpose(bz.f[:, 128:128 + GP], sfc[0:GP, :],
                                                  c_idf[0:GP, 0:GP]), sPE)
                W("scalar", sPE)
                W("scalar", sDVE)
                I("scalar", lambda e: e.copy(zsc, bz.f[:, 0:GP]), sACT)
                I("scalar", lambda e: e.copy(ssc, bz.f[:, 128:128 + GP]), sACT)
                bz.rd(sACT)
                W("vector", dQ)
                W("vector", sACT)
                if ncb >= 1:
                    W("vector", sPE, tp_done)
                for q in range(nq):
                    for j in range(8):
                        I("vector", lambda e, o=w4q[:, j::8], qs=q * qq, jj=j:
                          e.tensor_scalar(o, qbuf[:, qs:qs + qq], 4 * jj, 15,
                                          ALU.logical_shift_right, ALU.bitwise_and),
                          sDVE, ss=(j == 0))
                    g0 = q * (GP // nq)
                    for gl in range(GP // nq):
                        g = g0 + gl
                        I("vector", lambda e, gg=g, gloc=gl:
                          e.tensor_scalar(wbfT[:, gg * 128:(gg + 1) * 128],
                                          w4q[:, gloc * 128:(gloc + 1) * 128],
                                          ssc[:, gg:gg + 1], zsc[:, gg:gg + 1],
                                          ALU.mult, ALU.subtract), sDVE,
                          ss=(gl == 0))
                q_war_dve = sDVE.val
                W("scalar", dSO)
                g = 0
                while g < GP:
                    nb = min(4, GP - g)
                    bk = B[6 + ((g // 4) % 2)]
                    bk.pre_write()
                    W("tensor", sDVE)
                    for sl in range(nb):
                        I("tensor", lambda e, gg=g + sl, s_=sl,
                          o=bk.b[:, (sl) * 128:(sl + 1) * 128]:
                          e.transpose(o, wbfT[:, gg * 128:(gg + 1) * 128],
                                      c_idb[:, :]),
                          sPE if sl == nb - 1 else None)
                    W("scalar", sPE)
                    I("scalar", lambda e, gg=g, w_=nb * 128, i_=bk.b[:, 0:nb * 128]:
                      e.copy(stage[:, gg * 128:gg * 128 + w_], i_), sACT,
                      ss=(g < 8))
                    bk.rd(sACT)
                    g += nb
                tp_done = sPE.val
                q_war_pe = sPE.val
                W("gpsimd", sACT)
                dma("gpsimd", scratch_w[ncb], stage, dSO)

        set_phase("p_dq")
        emit_dequant(i_qwg, i_zg, i_sg, sWg, FC, KC)
        emit_dequant(i_qwu, i_zu, i_su, sWu, FC, KC)
        emit_dequant(i_qwd, i_zd, i_sd, sWd, KC, FC)

        for e in ENGS:
            gate_all(e)

        state = {"xsl_war": {}, "s52_war": {}}

        def ln_stats_pass(src):
            set_phase("ln")
            B[0].pre_write()
            B[1].pre_write()
            W("sync", dXW[0])
            W("sync", dXW[1])
            W("vector", sPE)
            W("scalar", sPE)
            xw = dict(state["xsl_war"])
            mmv = 0
            for kc in range(KC):
                pb = kc % 2
                if pb in xw:
                    W("sync", sDVE, xw[pb][0])
                    if xw[pb][1]:
                        W("sync", sACT, xw[pb][1])
                dma("sync", xsl[pb][:, :], src[kc * 128:(kc + 1) * 128, :], dXS[pb])
                W("vector", dXS[pb])
                if kc >= 1:
                    W("vector", sPE, mmv)
                I("vector", lambda e, p=pb: e.tensor_copy(xbf[:, :], xsl[p][:, :]),
                  sDVE)
                W("scalar", dXS[pb])
                if kc >= 1:
                    W("scalar", sPE, mmv)
                I("scalar", lambda e, p=pb: e.activation(
                    sqbf[:, :], xsl[p][:, :], AF.Square), sACT)
                xw[pb] = (sDVE.val, sACT.val)
                W("tensor", sDVE)
                W("tensor", sACT)
                I("tensor", lambda e, st=(kc == 0), sp=(kc == KC - 1):
                  e.matmul(B[0].f[0:1, :], c_onescol[:, :], xbf[:, :],
                           start=st, stop=sp), sPE)
                I("tensor", lambda e, st=(kc == 0), sp=(kc == KC - 1):
                  e.matmul(B[1].f[0:1, :], c_onescol[:, :], sqbf[:, :],
                           start=st, stop=sp), sPE)
                mmv = sPE.val
            state["xsl_war"] = xw

        def ln_rows_and_bcast():
            set_phase("ln")
            W("scalar", sPE)
            I("scalar", lambda e: e.activation(rowA, B[0].f[0:1, :], AF.Identity,
                                               scale=1.0 / HID), sACT)
            I("scalar", lambda e: e.activation(rowB, B[0].f[0:1, :], AF.Square,
                                               scale=1.0 / HID), sACT)
            B[0].rd(sACT)
            W("vector", sACT)
            I("vector", lambda e: e.scalar_tensor_tensor(
                rowB, B[1].f[0:1, :], 1.0 / HID, rowB, ALU.mult, ALU.subtract), sDVE)
            B[1].rd(sDVE)
            W("scalar", sDVE)
            I("scalar", lambda e: e.activation(rowB, rowB, AF.Ln, bias=c_eps[0:1, 0:1]), sACT)
            I("scalar", lambda e: e.activation(rowB, rowB, AF.Exp, scale=-0.5), sACT)
            I("scalar", lambda e: e.activation(rows_abf, rowB, AF.Identity), sACT)
            W("vector", sACT)
            I("vector", lambda e: e.scalar_tensor_tensor(
                rowA, rowA, -1.0, rowB, ALU.mult, ALU.mult), sDVE)
            W("scalar", sDVE)
            I("scalar", lambda e: e.activation(rows_bbf, rowA, AF.Identity), sACT)
            B[2].pre_write()
            B[3].pre_write()
            W("tensor", sACT)
            I("tensor", lambda e: e.matmul(B[2].f[:, :], c_onesb[:, :], rows_abf,
                                           start=True, stop=True), sPE)
            I("tensor", lambda e: e.matmul(B[3].f[:, :], c_onesb[:, :], rows_bbf,
                                           start=True, stop=True), sPE)

        def ln_build_xn(src, xn_free_pe):
            set_phase("ln")
            W("vector", sPE, xn_free_pe)
            W("sync", dXW[0])
            W("sync", dXW[1])
            xnv = []
            xw = dict(state["xsl_war"])
            for kc in range(KC):
                pb = kc % 2
                if pb in xw:
                    W("sync", sDVE, xw[pb][0])
                dma("sync", xsl[pb][:, :], src[kc * 128:(kc + 1) * 128, :], dXS[pb])
                W("vector", dXS[pb])
                W("vector", sPE)
                I("vector", lambda e, p=pb: e.tensor_tensor(
                    s52[p][:, :], xsl[p][:, :], B[2].f[:, :], ALU.mult), sDVE,
                  ss=(kc < 2))
                I("vector", lambda e, k=kc, p=pb: e.tensor_tensor(
                    xn[:, k * TT:(k + 1) * TT], s52[p][:, :], B[3].f[:, :],
                    ALU.add), sDVE)
                xw[pb] = (sDVE.val, 0)
                xnv.append(sDVE.val)
            B[2].rd(sDVE)
            B[3].rd(sDVE)
            state["xsl_war"] = xw
            return xnv

        def residual_evac(bank, mc, dst_sl, scale, fuse_stats, nst):
            pb = mc % 2
            W("vector", dXS[pb])
            W("vector", sPE)
            if scale == 1.0:
                I("vector", lambda e: e.tensor_tensor(
                    s52[pb][:, :], xsl[pb][:, :], bank.f[:, :], ALU.add), sDVE)
            else:
                I("vector", lambda e: e.scalar_tensor_tensor(
                    s52[pb][:, :], bank.f[:, :], scale, xsl[pb][:, :],
                    ALU.mult, ALU.add), sDVE)
            bank.rd(sDVE)
            W("gpsimd", sDVE)
            dma("gpsimd", dst_sl[mc * 128:(mc + 1) * 128, :], s52[pb][:, :], dXW[pb])
            if fuse_stats:
                if mc == 0:
                    B[0].pre_write()
                    B[1].pre_write()
                if mc >= 1:
                    W("vector", sPE, nst[0])
                I("vector", lambda e, p=pb: e.tensor_copy(xbf[:, :], s52[p][:, :]),
                  sDVE)
                if mc >= 1:
                    W("scalar", sPE, nst[0])
                W("scalar", sDVE)
                I("scalar", lambda e, p=pb: e.activation(
                    sqbf[:, :], s52[p][:, :], AF.Square), sACT)
                W("tensor", sDVE)
                W("tensor", sACT)
                I("tensor", lambda e, st=(mc == 0), sp=(mc == KC - 1):
                  e.matmul(B[0].f[0:1, :], c_onescol[:, :], xbf[:, :],
                           start=st, stop=sp), sPE)
                I("tensor", lambda e, st=(mc == 0), sp=(mc == KC - 1):
                  e.matmul(B[1].f[0:1, :], c_onescol[:, :], sqbf[:, :],
                           start=st, stop=sp), sPE)
                nst[0] = sPE.val
            return (sDVE.val, 0)

        def emit_ffn(t0, src, dst_sl, fuse_stats, xnv=None):
            set_phase("ffn_gu")
            guw = {}
            for fc in range(FC):
                pb = fc % 2
                gB, uB = B[4 + pb], B[6 + pb]
                gsl = war[:, pb * 4096: pb * 4096 + 4096]
                usl = war[:, 8192 + pb * 4096: 8192 + pb * 4096 + 4096]
                if pb in guw:
                    W("sync", sPE, guw[pb])
                dma("sync", gsl, sWg[fc], dWG[pb])
                dma("sync", usl, sWu[fc], dWU[pb])
                gB.pre_write()
                uB.pre_write()
                W("tensor", dWG[pb])
                W("tensor", dWU[pb])
                mm_accum(gB, [gsl[:, k * 128:(k + 1) * 128] for k in range(KC)],
                         [xn[:, k * TT:(k + 1) * TT] for k in range(KC)],
                         kwaits=(xnv if fc == 0 else None))
                mm_accum(uB, [usl[:, k * 128:(k + 1) * 128] for k in range(KC)],
                         [xn[:, k * TT:(k + 1) * TT] for k in range(KC)])
                guw[pb] = sPE.val
                hsl = har[:, fc * TT:(fc + 1) * TT]
                W("scalar", sPE)
                I("scalar", lambda e, o=hsl, b_=gB: e.activation(o, b_.f[:, :],
                                                                 AF.Silu), sACT)
                gB.rd(sACT)
                W("vector", sACT)
                I("vector", lambda e, o=hsl, b_=uB: e.tensor_tensor(
                    o, o, b_.f[:, :], ALU.mult), sDVE)
                uB.rd(sDVE)
            h_done = sDVE.val
            set_phase("ffn_dn")
            dsw = {}
            nst = [0]
            QW = 2752
            dsw = {}
            for mc in range(KC):
                pb = mc % 2
                dB = B[4 + pb]
                bufs = []
                for half in range(2):
                    th = (2 * mc + half) % 3
                    dsl = war[:, th * 5504: th * 5504 + 5504]
                    bufs.append(dsl)
                    if th in dsw:
                        W("sync", sPE, dsw[th])
                    elif mc == 0 and half == 0:
                        W("sync", sPE, guw[0])
                        W("sync", sPE, guw[1])
                    dma("sync", dsl[:, 0:2752], sWd[mc][:, half * 5504:half * 5504 + 2752], dWD[th])
                    dma("sync", dsl[:, 2752:5504],
                        sWd[mc][:, half * 5504 + 2752:half * 5504 + 5504], dWD[th])
                xwp = state["xsl_war"].get(pb)
                if xwp is not None:
                    W("sync", sDVE, xwp[0])
                dma("sync", xsl[pb][:, :], src[mc * 128:(mc + 1) * 128, t0:t0 + TT],
                    dXS[pb])
                dB.pre_write()
                W("tensor", dWD[(2 * mc) % 3])
                W("tensor", dWD[(2 * mc + 1) % 3])
                W("tensor", sDVE, h_done)
                lhs, rhs = [], []
                for k in range(FC):
                    half = k // 43
                    kk = k % 43
                    lhs.append(bufs[half][:, kk * 128:(kk + 1) * 128])
                    rhs.append(har[:, k * TT:(k + 1) * TT])
                mm_accum(dB, lhs, rhs)
                dsw[(2 * mc) % 3] = sPE.val
                dsw[(2 * mc + 1) % 3] = sPE.val
                v = residual_evac(dB, mc, dst_sl, 0.5, fuse_stats, nst)
                state["xsl_war"][pb] = v

        def emit_inproj(t0, ssm_prev_loads, xnv=None):
            set_phase("inproj")
            W("gpsimd", dHS)
            W("gpsimd", dSG)
            ipw = {}
            gw_war = {}
            for mc in range(MCI):
                pb = mc % 2
                th = mc % 3
                pB = B[4 + pb]
                wsl = war[:, th * 4096: th * 4096 + 4096]
                if th in ipw:
                    W("sync", sPE, ipw[th])
                dma("sync", wsl, sWin[mc], dWI[th])
                pB.pre_write()
                W("tensor", dWI[th])
                mm_accum(pB, [wsl[:, k * 128:(k + 1) * 128] for k in range(KC)],
                         [xn[:, k * TT:(k + 1) * TT] for k in range(KC)],
                         kwaits=(xnv if mc == 0 else None))
                ipw[th] = sPE.val
                if mc < CCN:
                    stg = xbf if (mc % 2 == 0) else sqbf
                    if mc % 2 in gw_war:
                        W("scalar", dGS[mc % 2])
                    W("scalar", sPE)
                    I("scalar", lambda e, o=stg[:, :], b_=pB:
                      e.activation(o, b_.f[:, :], AF.Silu), sACT)
                    pB.rd(sACT)
                    W("gpsimd", sACT)
                    dma("gpsimd", sGATE[mc], stg[:, :], dGS[mc % 2])
                    gw_war[mc % 2] = 1
                elif mc < CCN + CVC:
                    cc = mc - CCN
                    tl_r = tails[(t0 // TT) % 2]
                    tl_w = tails[((t0 // TT) + 1) % 2]
                    W("vector", sPE)
                    I("vector", lambda e, b_=pB: e.tensor_copy(convin[:, 3:515],
                                                               b_.f[:, :]), sDVE)
                    I("vector", lambda e, c=cc, t_=tl_r: e.tensor_copy(
                        convin[:, 0:3], t_[:, c * 3:(c + 1) * 3]), sDVE)
                    I("vector", lambda e, c=cc, t_=tl_w, b_=pB: e.tensor_copy(
                        t_[:, c * 3:(c + 1) * 3], b_.f[:, 509:512]), sDVE)
                    pB.rd(sDVE)
                    I("vector", lambda e, c=cc: e.tensor_scalar(
                        cacc[:, :], convin[:, 0:512],
                        c_convw[:, c * 4:c * 4 + 1], None, ALU.mult), sDVE)
                    for j in range(1, 4):
                        I("vector", lambda e, c=cc, jj=j: e.scalar_tensor_tensor(
                            cacc[:, :], convin[:, jj:512 + jj],
                            c_convw[:, c * 4 + jj:c * 4 + jj + 1], cacc[:, :],
                            ALU.mult, ALU.add), sDVE)
                    W("scalar", sDVE)
                    if cc < CCN:
                        stg = xbf if (mc % 2 == 0) else sqbf
                        if mc % 2 in gw_war:
                            W("scalar", dGS[mc % 2])
                        I("scalar", lambda e, c=cc, o=stg[:, :]: e.activation(
                            o, cacc[:, :], AF.Silu, bias=c_convb[:, c:c + 1]),
                          sACT)
                        W("gpsimd", sACT)
                        dma("gpsimd", sHS[cc], stg[:, :], dGS[mc % 2])
                        gw_war[mc % 2] = 1
                    else:
                        gidx = cc - CCN
                        dstt = bt_t if gidx < NG else ct_t
                        gg = gidx % NG
                        I("scalar", lambda e, c=cc, d_=dstt, g_=gg: e.activation(
                            d_[:, g_ * TT:(g_ + 1) * TT], cacc[:, :], AF.Silu,
                            bias=c_convb[:, c:c + 1]), sACT)
                else:
                    dsc = s52[0][:, :]
                    W("scalar", sPE)
                    W("scalar", sDVE)
                    I("scalar", lambda e, b_=pB: e.activation(
                        dsc, b_.f[:, :], AF.Abs, bias=c_dtb[:, 0:1]), sACT)
                    I("scalar", lambda e: e.activation(dsc, dsc, AF.Exp, scale=-1.0),
                      sACT)
                    I("scalar", lambda e: e.activation(dsc, dsc, AF.Ln, bias=1.0),
                      sACT)
                    I("scalar", lambda e, b_=pB: e.activation(
                        adt[:, :], b_.f[:, :], AF.Relu, bias=c_dtb[:, 0:1]), sACT)
                    pB.rd(sACT)
                    W("vector", sACT)
                    I("vector", lambda e: e.tensor_tensor(dt_f[:, :], dsc, adt[:, :],
                                                          ALU.add), sDVE)
                    I("vector", lambda e: e.tensor_scalar(dt_f[:, :], dt_f[:, :],
                                                          100.0, None, ALU.min), sDVE)
                    I("vector", lambda e: e.tensor_scalar(
                        adt[:, :], dt_f[:, :], c_eA[:, 0:1], -1.0, ALU.mult, ALU.mult),
                      sDVE)
            return sPE.val

        def emit_ssm(t0, st_out):
            hs_ch = har[:, 32768:40960]
            xch = xn[:, 0:8192]
            sg_ch = xn[:, 8192:16384]
            ysb = war[:, 0:8192]
            ltf = [war_f[:, 4096:4608], war_f[:, 4608:5120]]
            lte = [war[:, 10240:10752], war[:, 10752:11264]]
            mtb = [war[:, 11264:11776], war[:, 11776:12288]]
            ebc = [war_f[:, 6144:6656], war_f[:, 6656:7168]]
            cew = [war[:, 14336:14848], war[:, 14848:15360]]

            set_phase("ssm")
            gate_all("vector")
            gate_all("scalar")
            gate_all("tensor")
            st_war = {}
            facr_rd = 0
            loads_hi = 0
            for z in range(CPT):
                zg = (t0 // TT) * CPT + z
                zsl = slice(z * CH, (z + 1) * CH)
                for e in ("vector", "scalar", "tensor"):
                    W(e, sDVE)
                    W(e, sACT)
                    W(e, sPE)
                I("vector", lambda e, zs=zsl: e.tensor_tensor_scan(
                    acum, adt[:, zs], adt[:, zs], 0.0, ALU.add, ALU.bypass), sDVE)
                bk = B[6]
                bk.pre_write()
                W("tensor", sDVE)
                I("tensor", lambda e: e.transpose(bk.f[:, 0:128], acum, c_idf[:, :]),
                  sPE)
                W("scalar", sPE)
                I("scalar", lambda e: e.copy(acumT, bk.f[:, 0:128]), sACT)
                bk.rd(sACT)
                I("vector", lambda e: e.tensor_scalar(
                    tmp128, acum, acum[:, 127:128], None, ALU.subtract), sDVE)
                W("scalar", sDVE)
                I("scalar", lambda e: e.activation(tmp128, tmp128, AF.Exp, scale=-1.0),
                  sACT)
                bk2 = B[7]
                bk2.pre_write()
                W("tensor", sACT)
                I("tensor", lambda e: e.transpose(bk2.f[:, 0:128], tmp128,
                                                  c_idf[:, :]), sPE)
                W("scalar", sPE)
                I("scalar", lambda e: e.copy(decT, bk2.f[:, 0:128]), sACT)
                bk2.rd(sACT)
                bk.pre_write()
                I("tensor", lambda e, zs=zsl: e.transpose(bk.f[:, 0:128],
                                                          dt_f[:, zs], c_idf[:, :]),
                  sPE)
                W("scalar", sPE)
                I("scalar", lambda e: e.copy(dtT, bk.f[:, 0:128]), sACT)
                bk.rd(sACT)
                W("vector", sACT)
                I("vector", lambda e: e.tensor_tensor(coldec[:, :], dtT, decT,
                                                      ALU.mult), sDVE)
                bk2.pre_write()
                I("tensor", lambda e: e.transpose(bk2.f[0:1, 0:128], acum[:, 127:128],
                                                  c_idf[:, :]), sPE)
                W("scalar", sPE)
                I("scalar", lambda e: e.copy(rowC[0:1, 0:128], bk2.f[0:1, 0:128]),
                  sACT)
                bk2.rd(sACT)
                bk2.pre_write()
                W("tensor", sACT)
                I("tensor", lambda e: e.matmul(bk2.f[:, 0:128], c_onesf[:, :],
                                               rowC[0:1, 0:128], start=True,
                                               stop=True), sPE)
                W("scalar", sPE)
                I("scalar", lambda e: e.activation(eab[:, :], bk2.f[:, 0:128],
                                                   AF.Exp), sACT)
                bk2.rd(sACT)
                W("gpsimd", sDVE)
                W("gpsimd", dFA)
                dma("gpsimd", sAC[:, :], acum, dAC)
                W("sync", dGS[0])
                W("sync", dGS[1])
                W("sync", sDVE)
                W("sync", sACT)
                W("sync", sPE)
                dma("sync", hs_ch.rearrange("p (cc l) -> p cc l", l=CH),
                    sHS[:, :, zsl].rearrange("cc p l -> p cc l"), dHS)
                dma("sync", sg_ch.rearrange("p (cc l) -> p cc l", l=CH),
                    sGATE[:, :, zsl].rearrange("cc p l -> p cc l"), dSG)
                W("tensor", dHS)
                for cc in range(CCN):
                    bb = B[6 + cc % 2]
                    bb.pre_write()
                    I("tensor", lambda e, c=cc, o=bb.b[:, 0:128]:
                      e.transpose(o, hs_ch[:, c * 128:(c + 1) * 128], c_idb[:, :]),
                      sPE)
                    if cc % 2 == 0:
                        W("scalar", sPE)
                        I("scalar", lambda e, c=cc, i_=bb.b[:, 0:128]:
                          e.copy(xch[:, c * 128:(c + 1) * 128], i_), sACT,
                          ss=(cc < 2))
                        bb.rd(sACT)
                    else:
                        W("vector", sPE)
                        I("vector", lambda e, c=cc, i_=bb.b[:, 0:128]:
                          e.tensor_copy(xch[:, c * 128:(c + 1) * 128], i_), sDVE,
                          ss=(cc < 2))
                        bb.rd(sDVE)
                xch_v, xch_a = sDVE.val, sACT.val
                qt_war = {}
                for g in range(NG):
                    ssl = st_sl[g % 2]
                    if g % 2 in st_war:
                        W("sync", dSSw[g % 2])
                        W("sync", sDVE, st_war[g % 2])
                    if zg == 0:
                        W("gpsimd", sDVE, st_war.get(g % 2, 0))
                        W("gpsimd", dSSw[g % 2])
                        I("gpsimd", lambda e, s=ssl: e.memset(s[:, :], 0.0), sGC)
                        st_ld = (sGC, sGC.val)
                    else:
                        dma("sync", ssl[:, :], sST[:, g * 1024:(g + 1) * 1024],
                            dSSr[g % 2])
                        st_ld = (dSSr[g % 2], dSSr[g % 2].val)
                    W("gpsimd", dAC)
                    W("gpsimd", sPE, facr_rd)
                    facsrc = sAC.rearrange("p l -> (p l)").unsqueeze(0)
                    dma("gpsimd", facr, facsrc[:, g * 2048:(g + 1) * 2048], dFA)
                    bb = B[6 + g % 2]
                    bb.pre_write()
                    I("tensor", lambda e, o=bb.f[:, 0:128],
                      l_=bt_t[:, g * TT + z * CH: g * TT + z * CH + CH],
                      r_=ct_t[:, g * TT + z * CH: g * TT + z * CH + CH]:
                      e.matmul(o, l_, r_, start=True, stop=True), sPE)
                    W("vector", sPE)
                    I("vector", lambda e, b_=bb: e.tensor_tensor(
                        cbt[:, :], b_.f[:, 0:128], c_ut[:, :], ALU.mult), sDVE)
                    bb.rd(sDVE)
                    for q in range(4):
                        pq = q % 2
                        qb = B[2 + pq]
                        qb.pre_write()
                        W("tensor", dFA)
                        I("tensor", lambda e, qq=q, o=qb.f[:, :]:
                          e.matmul(o, c_onesf[:, :], facr[:, qq * 512:(qq + 1) * 512],
                                   start=True, stop=True), sPE)
                        h0 = g * 16 + q * 4
                        W("vector", sPE)
                        I("vector", lambda e, o=ltf[pq], b_=qb, hh=h0:
                          e.tensor_tensor(
                              o.rearrange("p (a b) -> p a b", a=4),
                              b_.f[:, :].rearrange("p (a b) -> p a b", a=4),
                              acumT[:, hh:hh + 4].unsqueeze(2).broadcast_to(
                                  [128, 4, 128]), ALU.subtract), sDVE)
                        I("vector", lambda e, o=ltf[pq]: e.tensor_scalar(
                            o, o, 0.0, None, ALU.min), sDVE)
                        W("scalar", sPE)
                        W("scalar", sDVE)
                        I("scalar", lambda e, o=ebc[pq], b_=qb:
                          e.activation(o, b_.f[:, :], AF.Exp), sACT)
                        qb.rd(sACT)
                        qb.rd(sDVE)
                        W("scalar", sDVE)
                        I("scalar", lambda e, o=lte[pq], i_=ltf[pq]:
                          e.activation(o, i_, AF.Exp), sACT)
                        W("vector", sACT)
                        for hh in range(4):
                            I("vector", lambda e, o=mtb[pq], i_=lte[pq], h=h0 + hh,
                              hl=hh: e.scalar_tensor_tensor(
                                  o[:, hl * 128:(hl + 1) * 128],
                                  i_[:, hl * 128:(hl + 1) * 128],
                                  dtT[:, h:h + 1], cbt[:, :], ALU.mult, ALU.mult),
                              sDVE, ss=(hh == 0))
                        I("vector", lambda e, o=cew[pq], eb_=ebc[pq],
                          ct_=ct_t[:, g * TT + z * CH: g * TT + z * CH + CH]
                          .unsqueeze(1).broadcast_to([128, 4, 128]):
                          e.tensor_tensor(
                              o.rearrange("p (a b) -> p a b", a=4), ct_,
                              eb_.rearrange("p (a b) -> p a b", a=4), ALU.mult), sDVE)
                        if zg > 0:
                            W("vector", st_ld[0], st_ld[1])
                            if pq in qt_war:
                                W("vector", sPE, qt_war[pq])
                            I("vector", lambda e, o=stbf[pq], s=ssl, qq=q:
                              e.tensor_copy(o[:, :], s[:, qq * 256:(qq + 1) * 256]),
                              sDVE)
                        yB = B[4 + (q // 2)]
                        if q == 0 or q == 2:
                            yB.pre_write()
                        W("tensor", sDVE)
                        for hh in range(4):
                            hglob = h0 + hh
                            off = ((hglob % 8)) * 64
                            I("tensor", lambda e, o=yB.f[:, :], m=mtb[pq], hl=hh,
                              hg=hglob, of=off, sp=(zg == 0):
                              e.matmul(o[:, of:of + 64],
                                       m[:, hl * 128:(hl + 1) * 128],
                                       xch[:, hg * 64:(hg + 1) * 64], start=True,
                                       stop=sp), sPE if zg == 0 else None)
                            if zg > 0:
                                I("tensor", lambda e, o=yB.f[:, :], c=cew[pq], hl=hh,
                                  sb_=stbf[pq], of=off:
                                  e.matmul(o[:, of:of + 64],
                                           c[:, hl * 128:(hl + 1) * 128],
                                           sb_[:, hl * 64:(hl + 1) * 64],
                                           start=False, stop=True), sPE)
                        qt_war[pq] = sPE.val
                        if q % 2 == 1:
                            W("scalar", sPE)
                            I("scalar", lambda e, gg=g, qh=q // 2, b_=yB:
                              e.copy(ysb[:, gg * 1024 + qh * 512:
                                         gg * 1024 + qh * 512 + 512], b_.f[:, :]),
                              sACT)
                            yB.rd(sACT)
                    facr_rd = sPE.val
                    W("vector", sDVE, xch_v)
                    W("vector", sACT, xch_a)
                    W("vector", sPE)
                    for hh in range(16):
                        h = g * 16 + hh
                        I("vector", lambda e, h_=h: e.tensor_scalar(
                            xch[:, h_ * 64:(h_ + 1) * 64],
                            xch[:, h_ * 64:(h_ + 1) * 64],
                            coldec[:, h_:h_ + 1], None, ALU.mult), sDVE,
                          ss=(hh == 0))
                    bb = B[6 + g % 2]
                    bb.pre_write()
                    W("tensor", sDVE)
                    I("tensor", lambda e, o=bb.b[:, 0:128],
                      l_=bt_t[:, g * TT + z * CH: g * TT + z * CH + CH]:
                      e.transpose(o, l_, c_idb[:, :]), sPE)
                    W("vector", sPE)
                    I("vector", lambda e, b_=bb: e.tensor_copy(b_lg[:, :],
                                                               b_.b[:, 0:128]), sDVE)
                    bb.rd(sDVE)
                    for half in range(2):
                        sb_ = B[6 + (g + 1) % 2]
                        sb_.pre_write()
                        W("tensor", sDVE)
                        I("tensor", lambda e, o=sb_.f[:, :], gg=g, hf=half:
                          e.matmul(o, b_lg[:, :],
                                   xch[:, (gg * 16 + hf * 8) * 64:
                                       (gg * 16 + hf * 8 + 8) * 64],
                                   start=True, stop=True), sPE)
                        W("vector", st_ld[0], st_ld[1])
                        W("vector", sPE)
                        for hh in range(8):
                            h = g * 16 + half * 8 + hh
                            I("vector", lambda e, h_=h, b2=sb_, hl=hh, s=ssl,
                              hf=half: e.scalar_tensor_tensor(
                                  s[:, (hf * 8 + hl) * 64:(hf * 8 + hl + 1) * 64],
                                  s[:, (hf * 8 + hl) * 64:(hf * 8 + hl + 1) * 64],
                                  eab[:, h_:h_ + 1],
                                  b2.f[:, hl * 64:(hl + 1) * 64],
                                  ALU.mult, ALU.add), sDVE, ss=(hh == 0))
                        sb_.rd(sDVE)
                    W("gpsimd", sDVE)
                    dma("gpsimd", sST[:, g * 1024:(g + 1) * 1024], ssl[:, :],
                        dSSw[g % 2])
                    st_war[g % 2] = sDVE.val
                    W("vector", dHS)
                    W("vector", dSG)
                    for hh2 in range(8):
                        cc = g * 8 + hh2
                        bb = B[6 + hh2 % 2]
                        bb.pre_write()
                        W("tensor", sACT)
                        I("tensor", lambda e, c=cc, o=bb.b[:, 0:128]:
                          e.transpose(o, ysb[:, c * 128:(c + 1) * 128], c_idb[:, :]),
                          sPE)
                        W("vector", sPE)
                        gt = g2h[hh2 % 2]
                        I("vector", lambda e, c=cc, o=gt[:, :], b_=bb:
                          e.scalar_tensor_tensor(o, hs_ch[:, c * 128:(c + 1) * 128],
                                                 c_D[:, c:c + 1], b_.b[:, 0:128],
                                                 ALU.mult, ALU.add), sDVE)
                        bb.rd(sDVE)
                        I("vector", lambda e, c=cc, g_=gt:
                          e.tensor_tensor(sg_ch[:, c * 128:(c + 1) * 128],
                                          sg_ch[:, c * 128:(c + 1) * 128],
                                          g_[:, :], ALU.mult), sDVE)
                    W("scalar", sDVE)
                    I("scalar", lambda e, gg=g: e.activation(
                        ysb[:, gg * 1024:(gg + 1) * 1024],
                        sg_ch[:, gg * 1024:(gg + 1) * 1024], AF.Square), sACT)
                    qb = B[2 + g % 2]
                    qb.pre_write()
                    W("tensor", sACT)
                    for c8 in range(8):
                        I("tensor", lambda e, q_=qb, gg=g, c_=c8:
                          e.matmul(q_.f[0:1, 0:128], c_onescol[:, :],
                                   ysb[:, gg * 1024 + c_ * 128:
                                       gg * 1024 + c_ * 128 + 128],
                                   start=(c_ == 0), stop=(c_ == 7)),
                          sPE if c8 == 7 else None)
                    W("scalar", sPE)
                    I("scalar", lambda e, b_=qb: e.activation(
                        rowC[0:1, 0:128], b_.f[0:1, 0:128], AF.Ln,
                        bias=c_eps[0:1, 0:1], scale=1.0 / 1024.0), sACT)
                    qb.rd(sACT)
                    I("scalar", lambda e: e.activation(rstd_bf, rowC[0:1, 0:128],
                                                       AF.Exp, scale=-0.5), sACT)
                    rb = B[2 + (g + 1) % 2]
                    rb.pre_write()
                    W("tensor", sACT)
                    I("tensor", lambda e, o=rb.f[:, 0:128]:
                      e.matmul(o, c_onesb[:, :], rstd_bf, start=True, stop=True), sPE)
                    W("vector", sPE)
                    for c8 in range(8):
                        cc = g * 8 + c8
                        I("vector", lambda e, c=cc, b_=rb,
                          o=har[:, cc * TT + z * CH: cc * TT + z * CH + CH]:
                          e.tensor_tensor(
                              o, sg_ch[:, c * 128:(c + 1) * 128], b_.f[:, 0:128],
                              ALU.mult), sDVE, ss=(c8 == 0))
                    rb.rd(sDVE)
            return 0

        def emit_outproj(t0, src, dst_sl):
            set_phase("outproj")
            gate_all("sync")
            opw = {}
            nst = [0]
            for mc in range(KC):
                pb = mc % 2
                oB = B[4 + pb]
                wsl = war[:, pb * 8192: pb * 8192 + 8192]
                if pb in opw:
                    W("sync", sPE, opw[pb])
                dma("sync", wsl[:, 0:4096], sWout[mc][:, 0:4096], dWO[pb])
                dma("sync", wsl[:, 4096:8192], sWout[mc][:, 4096:8192], dWO[pb])
                xwp = state["xsl_war"].get(pb)
                if xwp is not None:
                    W("sync", sDVE, xwp[0])
                dma("sync", xsl[pb][:, :], src[mc * 128:(mc + 1) * 128, t0:t0 + TT],
                    dXS[pb])
                oB.pre_write()
                W("tensor", dWO[pb])
                mm_accum(oB, [wsl[:, k * 128:(k + 1) * 128] for k in range(CCN)],
                         [har[:, k * TT:(k + 1) * TT] for k in range(CCN)])
                opw[pb] = sPE.val
                v = residual_evac(oB, mc, dst_sl, 1.0, True, nst)
                state["xsl_war"][pb] = v

        st_out = {}
        ssm_loads = 0
        for t in range(ntiles):
            t0 = t * TT
            if upto == "prologue":
                break
            ln_stats_pass(i_xT[:, t0:t0 + TT])
            ln_rows_and_bcast()
            gate_all("sync")
            xnv = ln_build_xn(i_xT[:, t0:t0 + TT], sPE.val)
            emit_ffn(t0, i_xT, sX[:, t0:t0 + TT], fuse_stats=True, xnv=xnv)
            if upto == "ffn1":
                break
            ln_rows_and_bcast()
            gate_all("sync")
            xnv = ln_build_xn(sX[:, t0:t0 + TT], sPE.val)
            ip_done = emit_inproj(t0, ssm_loads, xnv=xnv)
            if upto == "inproj":
                break
            ssm_loads = emit_ssm(t0, st_out)
            if upto == "ssm":
                break
            emit_outproj(t0, sX, sX[:, t0:t0 + TT])
            if upto == "mamba":
                break
            ln_rows_and_bcast()
            gate_all("sync")
            xnv = ln_build_xn(sX[:, t0:t0 + TT], sPE.val)
            emit_ffn(t0, sX, o_xT[:, t0:t0 + TT], fuse_stats=False, xnv=xnv)

        for s_ in DMASEMS:
            W("sync", s_)

        with nc.Block() as block:
            def mk(name):
                def body(eng):
                    cur = None
                    sid = None
                    for ph, fn in sched[name]:
                        if ph != cur:
                            if cur is not None:
                                nc.leave_named_scope(cur, sid, False)
                            sid, _ = nc.enter_named_scope(ph, False)
                            cur = ph
                        fn(eng)
                    if cur is not None:
                        nc.leave_named_scope(cur, sid, False)
                return body
            block.sync(mk("sync"))
            block.gpsimd(mk("gpsimd"))
            block.vector(mk("vector"))
            block.scalar(mk("scalar"))
            block.tensor(mk("tensor"))

    return nc


_BUILT = {}


def _prep(inputs):
    x = np.asarray(inputs["L0_attn_x"])
    xT = np.ascontiguousarray(np.transpose(x, (0, 2, 1)))
    conv_w = np.asarray(inputs["L0_attn_conv_w"]).reshape(CONVD, 4)
    convw = np.ascontiguousarray(
        conv_w.reshape(CVC, 128, 4).transpose(1, 0, 2).reshape(128, CVC * 4))
    convb = np.ascontiguousarray(
        np.asarray(inputs["L0_attn_conv_b"]).reshape(CVC, 128).T)
    nw = np.ascontiguousarray(
        np.asarray(inputs["L0_attn_norm_w"]).reshape(CCN, 128).T)
    Dr = np.ascontiguousarray(
        np.repeat(np.asarray(inputs["L0_attn_D"]), PD).reshape(CCN, 128).T)
    shared = {
        "i_qwg": np.ascontiguousarray(np.asarray(inputs["L0_ffn_gate_qweight"]).T),
        "i_qwu": np.ascontiguousarray(np.asarray(inputs["L0_ffn_up_qweight"]).T),
        "i_qwd": np.ascontiguousarray(np.asarray(inputs["L0_ffn_down_qweight"]).T),
        "i_sg": np.ascontiguousarray(inputs["L0_ffn_gate_scales"]),
        "i_su": np.ascontiguousarray(inputs["L0_ffn_up_scales"]),
        "i_sd": np.ascontiguousarray(inputs["L0_ffn_down_scales"]),
        "i_zg": np.ascontiguousarray(inputs["L0_ffn_gate_qzeros"]),
        "i_zu": np.ascontiguousarray(inputs["L0_ffn_up_qzeros"]),
        "i_zd": np.ascontiguousarray(inputs["L0_ffn_down_qzeros"]),
        "i_winT": np.ascontiguousarray(np.asarray(inputs["L0_attn_in_proj_w"]).T),
        "i_woutT": np.ascontiguousarray(np.asarray(inputs["L0_attn_out_proj_w"]).T),
        "i_convw": convw.astype(np.float32),
        "i_convb": convb.astype(np.float32),
        "i_nw": nw.astype(np.float32),
        "i_D": Dr.astype(np.float32),
        "i_dtb": np.asarray(inputs["L0_attn_dt_bias"]).reshape(128, 1).astype(np.float32),
        "i_alog": np.asarray(inputs["L0_attn_A_log"]).reshape(128, 1).astype(np.float32),
        "i_idb": np.eye(128, dtype=ml_dtypes.bfloat16),
        "i_idf": np.eye(128, dtype=np.float32),
        "i_onesb": np.ones((1, 128), dtype=ml_dtypes.bfloat16),
        "i_onesf": np.ones((1, 128), dtype=np.float32),
        "i_onescol": np.ones((128, 1), dtype=ml_dtypes.bfloat16),
        "i_ut": np.triu(np.ones((128, 128), np.float32)).astype(ml_dtypes.bfloat16),
        "i_eps": np.full((128, 1), EPS, np.float32),
    }
    in_maps = []
    for c in range(NCORES):
        m = dict(shared)
        m["i_xT"] = np.ascontiguousarray(xT[c])
        in_maps.append(m)
    return in_maps


def kernel(**inputs):
    in_maps = _prep(inputs)
    if "nc" not in _BUILT:
        _BUILT["nc"] = build_program()
    nc = _BUILT["nc"]
    res = run_bass_kernel_spmd(nc, in_maps, core_ids=list(range(NCORES)))
    out = np.stack([np.asarray(r["o_xT"], dtype=np.float32).T for r in res.results])
    return np.ascontiguousarray(out)
